# version 16
# baseline (speedup 1.0000x reference)
import numpy as np
import ml_dtypes
from contextlib import ExitStack

import concourse.bass as bass
import concourse.mybir as mybir
from concourse.bass_utils import run_bass_kernel_spmd

F32 = mybir.dt.float32
BF16 = mybir.dt.bfloat16
AF = mybir.ActivationFunctionType
ALU = mybir.AluOpType
AX = mybir.AxisListType

NCORE = 8
TC = 1024
D = 2048
NE = 16
NT = 8
NI = 44
NH = 32
IT = 5632
DWQ = 1408
NSPLIT = 12

ENGS = ("sync", "gpsimd", "scalar", "vector", "tensor")
OWN_SEM = {"vector": "v", "scalar": "s"}


class Ctx:
    def __init__(self, dry, M=None):
        self.dry = dry
        self.M = M or {}
        self.ctr = {}
        self.ms = {}
        self.eng_ops = {e: [] for e in ENGS}

    def emit(self, eng, fn=None, waits=(), inc=None, ms=None):
        if inc is None and eng in OWN_SEM:
            inc = (OWN_SEM[eng], 1)
        if inc is not None:
            sem, amt = inc
            self.ctr[sem] = self.ctr.get(sem, 0) + amt
            if ms is not None:
                self.ms[ms] = (sem, self.ctr[sem])
        if not self.dry:
            rw = []
            for k in waits:
                if k is None:
                    continue
                if k not in self.M:
                    raise KeyError(f"milestone {k} not in pass-1 table")
                rw.append(self.M[k])
            self.eng_ops[eng].append((fn, tuple(rw), inc))

    def w(self, key):
        return key


def _body(engname, ops, sems):
    own = OWN_SEM.get(engname)

    def body(e):
        waited = {}
        own_count = 0
        for fn, waits, inc in ops:
            if own is not None and own_count > 0:
                if own_count > waited.get(own, 0):
                    e.wait_ge(sems[own], own_count)
                    waited[own] = own_count
            for semname, val in waits:
                if val <= waited.get(semname, 0):
                    continue
                e.wait_ge(sems[semname], val)
                waited[semname] = val
            instr = fn(e)
            if inc is not None:
                instr.then_inc(sems[inc[0]], inc[1])
                if own is not None and inc[0] == own:
                    own_count += inc[1]
    return body


def build(dry, M=None):
    nc = bass.Bass("TRN2", target_bir_lowering=False)
    ctx = Ctx(dry, M)

    ap = {}
    ap["x"] = nc.dram_tensor("x", [TC, D], F32, kind="ExternalInput").ap()
    ap["w1"] = nc.dram_tensor("w1", [NE, 128, D], F32, kind="ExternalInput").ap()
    ap["wdt"] = nc.dram_tensor("wdt", [NE, 128, NH], F32, kind="ExternalInput").ap()
    ap["bdt"] = nc.dram_tensor("bdt", [NH, 1], F32, kind="ExternalInput").ap()
    ap["alog"] = nc.dram_tensor("alog", [128, NE], F32, kind="ExternalInput").ap()
    ap["gw"] = nc.dram_tensor("gw", [NI, 128, D], F32, kind="ExternalInput").ap()
    ap["uw"] = nc.dram_tensor("uw", [NI, 128, D], F32, kind="ExternalInput").ap()
    ap["dw"] = nc.dram_tensor("dw", [NE, 128, IT], F32, kind="ExternalInput").ap()
    ap["wo"] = nc.dram_tensor("wo", [4, NE, 128, 512], F32, kind="ExternalInput").ap()
    ap["gsc"] = nc.dram_tensor("gsc", [128, NI], F32, kind="ExternalInput").ap()
    ap["usc"] = nc.dram_tensor("usc", [128, NI], F32, kind="ExternalInput").ap()
    ap["dsc"] = nc.dram_tensor("dsc", [128, NE], F32, kind="ExternalInput").ap()
    ap["msk"] = nc.dram_tensor("msk", [128, 128], F32, kind="ExternalInput").ap()
    ap["imsk"] = nc.dram_tensor("imsk", [128, 128], F32, kind="ExternalInput").ap()
    ap["ident"] = nc.dram_tensor("ident", [128, 128], F32, kind="ExternalInput").ap()
    ap["out"] = nc.dram_tensor("out", [TC, D], F32, kind="ExternalOutput").ap()
    sp_d = nc.dram_tensor("sp_d", [NH, TC], F32).ap()
    csp_d = nc.dram_tensor("csp_d", [NH, TC], F32).ap()
    yst_d = nc.dram_tensor("yst_d", [D, TC], BF16).ap()
    cc_in = nc.dram_tensor("cc_in", [128, 32], F32)
    cc_res = nc.dram_tensor("cc_res", [NCORE * 128, 32], F32)

    E = ctx.emit

    with ExitStack() as outer:
        def sb(name, shape, dt, stack=outer):
            return stack.enter_context(nc.sbuf_tensor(name, shape, dt))

        b_outer_holder = [None]
        xnT_holder = [None]
        gu_holder = [None]
        xnT_stack = ExitStack()
        xnT_holder[0] = xnT_stack
        xnT = xnT_stack.enter_context(nc.sbuf_tensor("xnT", [128, NE * TC], BF16, side="right"))
        hdnT_a = outer.enter_context(nc.sbuf_tensor("hdnT_a", [128, NSPLIT * TC], BF16))

        identf = sb("identf", [128, 128], F32)
        identb = sb("identb", [128, 128], BF16)
        msk_sb = sb("msk_sb", [128, 128], F32)
        imsk_sb = sb("imsk_sb", [128, 128], F32)
        alog_sb = sb("alog_sb", [128, NE], F32)
        aexp_sb = sb("aexp_sb", [128, NE], F32)
        acont = sb("acont", [128, NE], F32)
        gs_sb = sb("gs_sb", [128, NI], F32)
        us_sb = sb("us_sb", [128, NI], F32)
        ds_sb = sb("ds_sb", [128, NE], F32)
        bdt_sb = sb("bdt_sb", [NH, 1], F32)
        wdt_sb = sb("wdt_sb", [128, NE * NH], F32)
        hl_sb = sb("hl_sb", [128, NE], F32)
        alast = sb("alast", [128, NE], F32)
        achkp = sb("achkp", [128, NE], F32)
        achk = sb("achk", [128, NE], F32)
        hin = sb("hin", [128, NE], F32)
        wk16a = sb("wk16a", [128, NE], F32)
        wk16b = sb("wk16b", [128, NE], F32)
        carr = sb("carr", [128, 256], F32)
        red = sb("red", [128, NT], F32)
        negm = sb("negm", [128, NT], F32)
        ssq = sb("ssq", [128, NT], F32)
        veps = sb("veps", [128, NT], F32)
        lnv = sb("lnv", [128, NT], F32)
        rstd = sb("rstd", [128, NT], F32)
        gu_stack = ExitStack()
        gu_holder[0] = gu_stack
        gwb = [gu_stack.enter_context(nc.sbuf_tensor(f"gwb{b}", [128, D], BF16, side="right")) for b in range(3)]
        uwb = [gu_stack.enter_context(nc.sbuf_tensor(f"uwb{b}", [128, D], BF16, side="right")) for b in range(3)]
        w1b = [gu_stack.enter_context(nc.sbuf_tensor(f"w1b{b}", [128, D], BF16, side="right")) for b in range(2)]
        sgb = [gu_stack.enter_context(nc.sbuf_tensor(f"sgb{b}", [128, TC], BF16, side="right")) for b in range(2)]
        uub = [gu_stack.enter_context(nc.sbuf_tensor(f"uub{b}", [128, TC], BF16, side="right")) for b in range(2)]

        SEMNAMES = ("sml", "xt0s", "xt1s", "xt2s", "xt3s", "xt4s", "xt5s", "xt6s", "xt7s", "w1s", "sps", "als", "bc0s", "bc1s",
                    "bc2s", "cci", "cc", "car", "yb0", "yb1", "gw0", "gw1", "gw2", "uw0", "uw1", "uw2",
                    "dw0", "dw1", "xb0", "xb1", "ysl", "wq0", "wq1", "x20", "x21",
                    "os0", "os1", "os2", "os3", "dbs", "t", "v", "s")
        sems = {n: outer.enter_context(nc.semaphore(n)) for n in SEMNAMES}

        def here(sem):
            return (sem, ctx.ctr.get(sem, 0))

        def p6_load(i):
            E("gpsimd", (lambda i=i: (lambda e: e.dma_start(
                out=gwb[i % 3][:], in_=ap["gw"][i, :, :])))(),
              waits=[("gumm", i - 3) if i >= 3 else None],
              inc=(f"gw{i % 3}", 16), ms=("gdmah", i, 0))
            E("gpsimd", (lambda i=i: (lambda e: e.dma_start(
                out=uwb[i % 3][:], in_=ap["uw"][i, :, :])))(),
              waits=[("gumm", i - 3) if i >= 3 else None],
              inc=(f"uw{i % 3}", 16), ms=("udmah", i, 0))

        taps = []

        def reg_tap(name, src, shape, dt_, wait):
            taps.append((name, src, shape, dt_, wait))

        def finish():
            for name, src, shape, dt_, wait in taps:
                dout = nc.dram_tensor(name, shape, dt_, kind="ExternalOutput").ap()
                E("gpsimd", (lambda dout=dout, src=src: (lambda e: e.dma_start(
                    out=dout, in_=src)))(),
                  waits=[wait], inc=("dbs", 16))
            finals = []
            for sname in SEMNAMES:
                if sname in ("t", "v", "s", "cc"):
                    continue
                tot = ctx.ctr.get(sname, 0)
                if tot:
                    ctx.ms[("fin", sname)] = (sname, tot)
                    finals.append(("fin", sname))
            if finals:
                E("sync", lambda e: e.nop(), waits=finals)
                E("gpsimd", lambda e: e.nop(), waits=finals)
            if not dry:
                with nc.Block() as block:
                    block.sync(_body("sync", ctx.eng_ops["sync"], sems))
                    block.gpsimd(_body("gpsimd", ctx.eng_ops["gpsimd"], sems))
                    block.scalar(_body("scalar", ctx.eng_ops["scalar"], sems))
                    block.vector(_body("vector", ctx.eng_ops["vector"], sems))
                    block.tensor(_body("tensor", ctx.eng_ops["tensor"], sems))
            for st_ in (b_outer_holder, xnT_holder, gu_holder):
                if st_ and st_[0] is not None:
                    st_[0].close()
                    st_[0] = None
            return nc, ctx

        E("sync", lambda e: e.dma_start(out=identf[:], in_=ap["ident"]),
          inc=("sml", 16))
        E("sync", lambda e: e.dma_start(out=alog_sb[:], in_=ap["alog"]),
          inc=("sml", 16), ms=("identld",))

        def small_loads():
            small = [("msk", msk_sb[:], ap["msk"]),
                     ("imsk", imsk_sb[:], ap["imsk"]),
                     ("gs", gs_sb[:], ap["gsc"]), ("us", us_sb[:], ap["usc"]),
                     ("ds", ds_sb[:], ap["dsc"]), ("bdt", bdt_sb[:], ap["bdt"])]
            for name, dst, srcap in small:
                E("gpsimd", (lambda d=dst, s=srcap: (lambda e: e.dma_start(out=d, in_=s)))(),
                  inc=("sml", 16))
            for Ew in range(NE):
                E("gpsimd", (lambda Ew=Ew: (lambda e: e.dma_start(
                    out=wdt_sb[:, Ew * NH:(Ew + 1) * NH], in_=ap["wdt"][Ew, :, :])))(),
                  inc=("sml", 16))
            ctx.ms[("ldall",)] = here("sml")

        E("scalar", lambda e: e.copy(out=identb[:], in_=identf[:]),
          waits=[("identld",)], inc=("s", 1), ms=("identb",))
        E("scalar", lambda e: e.activation(out=aexp_sb[:], in_=alog_sb[:], func=AF.Exp),
          waits=[("identld",)], inc=("s", 1), ms=("aexpc",))
        E("vector", lambda e: e.tensor_scalar(out=acont[:], in0=aexp_sb[:], scalar1=-1.0,
                                              scalar2=None, op0=ALU.mult),
          waits=[("aexpc",)], inc=("v", 1), ms=("acont",))

        with ExitStack() as a1:
            xt = [a1.enter_context(nc.sbuf_tensor(f"xt{b}", [128, D], F32)) for b in range(NT)]
            sq = a1.enter_context(nc.sbuf_tensor("sq", [128, D], F32))
            av = a1.enter_context(nc.sbuf_tensor("av", [128, NT], F32))
            m2 = a1.enter_context(nc.sbuf_tensor("m2", [128, NT], F32))
            xnb = [a1.enter_context(nc.sbuf_tensor(f"xnb{b}", [128, D], BF16)) for b in range(2)]
            with nc.psum_tensor("ps_tr0", [128, 128], BF16) as pt0, \
                 nc.psum_tensor("ps_tr1", [128, 128], BF16) as pt1:
                ps_tr = [pt0, pt1]
                for tt in range(NT):
                    E("sync", (lambda tt=tt: (lambda e: e.dma_start(
                        out=xt[tt][:], in_=ap["x"][tt * 128:(tt + 1) * 128, :])))(),
                      inc=(f"xt{tt}s", 16), ms=("xload", tt))
                    E("vector", (lambda tt=tt: (lambda e: e.reduce_sum(
                        out=red[:, tt:tt + 1], in_=xt[tt][:], axis=AX.X)))(),
                      waits=[("xload", tt)], inc=("v", 1), ms=("red", tt))
                    E("scalar", (lambda tt=tt: (lambda e: e.activation(
                        out=sq[:], in_=xt[tt][:], func=AF.Square,
                        accum_out=ssq[:, tt:tt + 1])))(),
                      waits=[("xload", tt)], inc=("s", 1), ms=("ssq", tt))
                for tt in range(NT):
                    E("vector", (lambda tt=tt: (lambda e: e.tensor_scalar(
                        out=negm[:, tt:tt + 1], in0=red[:, tt:tt + 1],
                        scalar1=-1.0 / D, scalar2=None, op0=ALU.mult)))(),
                      waits=[("red", tt)], inc=("v", 1), ms=("negm", tt))
                    E("vector", (lambda tt=tt: (lambda e: e.tensor_scalar(
                        out=av[:, tt:tt + 1], in0=ssq[:, tt:tt + 1],
                        scalar1=1.0 / D, scalar2=1e-5, op0=ALU.mult, op1=ALU.add)))(),
                      waits=[("ssq", tt)], inc=("v", 1))
                    E("vector", (lambda tt=tt: (lambda e: e.tensor_tensor(
                        out=m2[:, tt:tt + 1], in0=negm[:, tt:tt + 1],
                        in1=negm[:, tt:tt + 1], op=ALU.mult)))(),
                      inc=("v", 1))
                    E("vector", (lambda tt=tt: (lambda e: e.tensor_sub(
                        out=veps[:, tt:tt + 1], in0=av[:, tt:tt + 1],
                        in1=m2[:, tt:tt + 1])))(),
                      inc=("v", 1), ms=("veps", tt))
                    E("scalar", (lambda tt=tt: (lambda e: e.activation(
                        out=lnv[:, tt:tt + 1], in_=veps[:, tt:tt + 1], func=AF.Ln)))(),
                      waits=[("veps", tt)], inc=("s", 1))
                    E("scalar", (lambda tt=tt: (lambda e: e.activation(
                        out=rstd[:, tt:tt + 1], in_=lnv[:, tt:tt + 1], func=AF.Exp,
                        scale=-0.5)))(),
                      inc=("s", 1), ms=("rstd", tt))
                for tt in range(NT):
                    E("vector", (lambda tt=tt: (lambda e: e.tensor_scalar(
                        out=xnb[tt % 2][:], in0=xt[tt][:], scalar1=negm[:, tt:tt + 1],
                        scalar2=rstd[:, tt:tt + 1], op0=ALU.add, op1=ALU.mult)))(),
                      waits=[("rstd", tt), ("trdone", tt - 2) if tt >= 2 else None],
                      inc=("v", 1), ms=("xnb", tt))
                    for d in range(NE):
                        k = (tt * NE + d) % 2
                        E("tensor", (lambda tt=tt, d=d, k=k: (lambda e: e.transpose(
                            out=ps_tr[k][:], in_=xnb[tt % 2][:, d * 128:(d + 1) * 128],
                            identity=identb[:])))(),
                          waits=[("xnb", tt), ("identb",), ("trcopy", tt, d - 2) if d >= 2
                                 else (("trcopy", tt - 1, NE - 2 + d) if tt >= 1 else None)],
                          inc=("t", 1), ms=("tr", tt, d))
                        E("vector", (lambda tt=tt, d=d, k=k: (lambda e: e.tensor_copy(
                            out=xnT[:, d * TC + tt * 128: d * TC + (tt + 1) * 128],
                            in_=ps_tr[k][:])))(),
                          waits=[("tr", tt, d)], inc=("v", 1), ms=("trcopy", tt, d))
                    ctx.ms[("trdone", tt)] = ctx.ms[("trcopy", tt, NE - 1)]
            ctx.ms[("xnT",)] = ctx.ms[("trcopy", NT - 1, NE - 1)]

        if _PH <= 1:
            reg_tap("dbg_red", red[:], [128, NT], F32, ("xnT",))
            reg_tap("dbg_negm", negm[:], [128, NT], F32, ("xnT",))
            reg_tap("dbg_ssq", ssq[:], [128, NT], F32, ("xnT",))
            reg_tap("dbg_veps", veps[:], [128, NT], F32, ("xnT",))
            reg_tap("dbg_rstd", rstd[:], [128, NT], F32, ("xnT",))
            reg_tap("dbg_xnt", xnT[:], [128, NE * TC], BF16, ("xnT",))
            return finish()

        u_stack = ExitStack()
        with ExitStack() as a2:
            u_sb = u_stack.enter_context(nc.sbuf_tensor("u_sb", [128, NE * TC], F32))
            u_bf = u_sb.bitcast(BF16)
            with ExitStack() as a2w:
                zch = [a2w.enter_context(nc.sbuf_tensor(f"zch{i}", [NH, TC], F32)) for i in range(4)]
                z_sb, zs1, zs2, zs3 = zch
                csp_sb = a2w.enter_context(nc.sbuf_tensor("csp_sb", [NH, TC], F32))

                with nc.psum_tensor("pmm0", [128, TC], F32) as pm0, \
                     nc.psum_tensor("pmm1", [128, TC], F32) as pm1, \
                     nc.psum_tensor("pdt", [NH, TC], F32) as pdt:
                    pmm = [pm0, pm1]
                    p6_load(0)
                    p6_load(1)
                    for Et in range(NE):
                        if Et == 2:
                            small_loads()
                        E("gpsimd", (lambda Et=Et: (lambda e: e.dma_start(
                            out=w1b[Et % 2][:], in_=ap["w1"][Et, :, :])))(),
                          waits=[("mm1", Et - 2) if Et >= 2 else None],
                          inc=(f"gw{Et % 2}", 16), ms=("w1dma", Et))
                        for nh in range(2):
                            for dd in range(NE):
                                first = (nh == 0 and dd == 0)
                                last = (nh == 1 and dd == NE - 1)
                                E("tensor", (lambda Et=Et, nh=nh, dd=dd: (lambda e: e.matmul(
                                    out=pmm[Et % 2][:, nh * 512:(nh + 1) * 512],
                                    lhsT=w1b[Et % 2][:, dd * 128:(dd + 1) * 128],
                                    rhs=xnT[:, dd * TC + nh * 512: dd * TC + nh * 512 + 512],
                                    start=(dd == 0), stop=(dd == NE - 1))))(),
                                  waits=([("w1dma", Et), ("xnT",),
                                          ("uev", Et - 2) if Et >= 2 else None]
                                         if first else []),
                                  inc=(("t", 1) if last else None),
                                  ms=(("mm1", Et) if last else None))
                        E("vector", (lambda Et=Et: (lambda e: e.tensor_copy(
                            out=u_sb[:, Et * TC:(Et + 1) * TC], in_=pmm[Et % 2][:])))(),
                          waits=[("mm1", Et)], inc=("v", 1), ms=("uev", Et))
                    for nh in range(2):
                        for Et in range(NE):
                            first = (nh == 0 and Et == 0)
                            last = (nh == 1 and Et == NE - 1)
                            E("tensor", (lambda Et=Et, nh=nh: (lambda e: e.matmul(
                                out=pdt[:, nh * 512:(nh + 1) * 512],
                                lhsT=wdt_sb[:, Et * NH:(Et + 1) * NH],
                                rhs=u_sb[:, Et * TC + nh * 512: Et * TC + nh * 512 + 512],
                                start=(Et == 0), stop=(Et == NE - 1))))(),
                              waits=([("uev", NE - 1), ("ldall",)] if first else []),
                              inc=(("t", 1) if last else None),
                              ms=(("dtmm",) if last else None))
                    E("vector", lambda e: e.tensor_scalar(out=z_sb[:], in0=pdt[:],
                                                          scalar1=bdt_sb[:, 0:1],
                                                          scalar2=None, op0=ALU.add),
                      waits=[("dtmm",), ("ldall",)], inc=("v", 1), ms=("z",))
                E("vector", lambda e: e.tensor_scalar(out=zs1[:], in0=z_sb[:], scalar1=-1.0,
                                                      scalar2=None, op0=ALU.mult),
                  waits=[("z",)], inc=("v", 1))
                E("vector", lambda e: e.tensor_tensor(out=zs2[:], in0=z_sb[:], in1=zs1[:],
                                                      op=ALU.min),
                  inc=("v", 1))
                E("vector", lambda e: e.tensor_scalar(out=zs1[:], in0=z_sb[:], scalar1=0.0,
                                                      scalar2=None, op0=ALU.max),
                  inc=("v", 1), ms=("zp",))
                E("scalar", lambda e: e.activation(out=zs3[:], in_=zs2[:], func=AF.Exp),
                  waits=[("zp",)], inc=("s", 1))
                E("scalar", lambda e: e.activation(out=zs2[:], in_=zs3[:], func=AF.Ln, bias=1.0),
                  inc=("s", 1), ms=("l1",))
                E("vector", lambda e: e.tensor_tensor(out=z_sb[:], in0=zs1[:], in1=zs2[:],
                                                      op=ALU.add),
                  waits=[("l1",)], inc=("v", 1), ms=("sp",))
                E("vector", lambda e: e.memset(zs3[:], 0.0), inc=("v", 1))
                E("vector", lambda e: e.tensor_tensor_scan(out=csp_sb[:], data0=z_sb[:],
                                                           data1=zs3[:], initial=0.0,
                                                           op0=ALU.add, op1=ALU.add),
                  inc=("v", 1), ms=("csp",))
                E("sync", lambda e: e.dma_start(out=sp_d, in_=z_sb[:]),
                  waits=[("sp",)], inc=("sps", 16))
                E("sync", lambda e: e.dma_start(out=csp_d, in_=csp_sb[:]),
                  waits=[("csp",)], inc=("sps", 16))
                ctx.ms[("spst",)] = ctx.ms[("cspst",)] = here("sps")
                for j in range(NE):
                    for h2 in range(2):
                        hidx = 2 * j + h2
                        dst = alast[0:64, j:j + 1] if h2 == 0 else alast[64:128, j:j + 1]
                        E("gpsimd", (lambda dst=dst, hidx=hidx: (lambda e: e.dma_start(
                            out=dst, in_=csp_d[hidx:hidx + 1, TC - 1:TC].broadcast_to([64, 1]))))(),
                          waits=[("cspst",)], inc=("als", 16))
                ctx.ms[("alast",)] = here("als")
                E("vector", lambda e: e.tensor_tensor(out=achkp[:], in0=acont[:], in1=alast[:],
                                                      op=ALU.mult),
                  waits=[("alast",), ("acont",)], inc=("v", 1), ms=("achkp",))
                E("scalar", lambda e: e.activation(out=achk[:], in_=achkp[:], func=AF.Exp),
                  waits=[("achkp",)], inc=("s", 1), ms=("achk",))

            pgpu = ExitStack()
            pg = [pgpu.enter_context(nc.psum_tensor(f"pg{b}", [128, TC], F32)) for b in range(2)]
            pu = [pgpu.enter_context(nc.psum_tensor(f"pu{b}", [128, TC], F32)) for b in range(2)]

            def hdn_slice(i):
                if i < NSPLIT:
                    return hdnT_a[:, i * TC:(i + 1) * TC]
                m = i - NSPLIT
                return u_bf[:, m * TC:(m + 1) * TC]

            def p6_compute(i):
                for which, psum, wsb in (("g", pg, gwb), ("u", pu, uwb)):
                    for nh in range(2):
                        for hh in range(NE):
                            first = (which == "g" and nh == 0 and hh == 0)
                            firstu = (which == "u" and nh == 0 and hh == 0)
                            E("tensor", (lambda i=i, which=which, nh=nh, hh=hh, psum=psum, wsb=wsb:
                                         (lambda e: e.matmul(
                                out=psum[i % 2][:, nh * 512:(nh + 1) * 512],
                                lhsT=wsb[i % 3][:, hh * 128:(hh + 1) * 128],
                                rhs=xnT[:, hh * TC + nh * 512: hh * TC + nh * 512 + 512],
                                start=(hh == 0), stop=(hh == NE - 1))))(),
                              waits=([("gdmah", i, 0), ("udmah", i, 0),
                                      ("silu", i - 2) if i >= 2 else None,
                                      ("uu", i - 2) if i >= 2 else None,
                                      ("uev", NE - 1) if i == 0 else None]
                                     if first else
                                     ([("z",)] if (firstu and i == 0) else [])),
                              inc=(("t", 1) if (nh == 1 and hh == NE - 1) else None),
                              ms=((("gmm", i) if which == "g" else ("gumm", i))
                                  if (nh == 1 and hh == NE - 1) else None))
                E("scalar", (lambda i=i: (lambda e: e.activation(
                    out=sgb[i % 2][:], in_=pg[i % 2][:], func=AF.Silu,
                    scale=gs_sb[:, i:i + 1])))(),
                  waits=[("gmm", i), ("ldall",), ("hdn", i - 2) if i >= 2 else None],
                  inc=("s", 1), ms=("silu", i))
                E("vector", (lambda i=i: (lambda e: e.tensor_scalar(
                    out=uub[i % 2][:], in0=pu[i % 2][:], scalar1=us_sb[:, i:i + 1],
                    scalar2=None, op0=ALU.mult)))(),
                  waits=[("gumm", i), ("ldall",), ("hdn", i - 2) if i >= 2 else None],
                  inc=("v", 1), ms=("uu", i))
                E("vector", (lambda i=i: (lambda e: e.tensor_tensor(
                    out=hdn_slice(i), in0=sgb[i % 2][:], in1=uub[i % 2][:],
                    op=ALU.mult)))(),
                  waits=[("silu", i),
                         ("corr", (i - NSPLIT) // 2) if i >= NSPLIT else None],
                  inc=("v", 1), ms=("hdn", i))

            if _PH <= 2:
                reg_tap("dbg_u", u_sb[:], [128, NE * TC], F32, ("uev", NE - 1))
                reg_tap("dbg_z", z_sb[:], [NH, TC], F32, ("z",))
                reg_tap("dbg_sp", sp_d, [NH, TC], F32, ("spst",))
                reg_tap("dbg_csp", csp_d, [NH, TC], F32, ("cspst",))
                reg_tap("dbg_xnt", xnT[:], [128, NE * TC], BF16, ("xnT",))
                return finish()

            with ExitStack() as a3:
                bc = [a3.enter_context(nc.sbuf_tensor(f"bc{b}", [128, TC], F32)) for b in range(3)]
                a_sc = [a3.enter_context(nc.sbuf_tensor(f"a_sc{b}", [128, TC], F32)) for b in range(2)]
                scr = [a3.enter_context(nc.sbuf_tensor("scr0", [128, TC], F32))]
                ysbf = [a3.enter_context(nc.sbuf_tensor(f"ysbf{b}", [128, TC], BF16)) for b in range(2)]

                def p3_step(j):
                    for h2 in range(2):
                        dst = bc[j % 3][0:64, :] if h2 == 0 else bc[j % 3][64:128, :]
                        E("sync", (lambda dst=dst, j=j, h2=h2: (lambda e: e.dma_start(
                            out=dst, in_=sp_d[2 * j + h2:2 * j + h2 + 1, :].broadcast_to([64, TC]))))(),
                          waits=[("spst",), ("aexp", j - 3) if j >= 3 else None],
                          inc=(f"bc{j % 3}s", 16))
                    ctx.ms[("bca", j)] = here(f"bc{j % 3}s")
                    E("scalar", (lambda j=j: (lambda e: e.activation(
                        out=a_sc[j % 2][:], in_=bc[j % 3][:], func=AF.Exp,
                        scale=acont[:, j:j + 1])))(),
                      waits=[("bca", j), ("acont",), ("scan", j - 2) if j >= 2 else None],
                      inc=("s", 1), ms=("aexp", j))
                    E("vector", (lambda j=j: (lambda e: e.tensor_tensor_scan(
                        out=scr[0][:], data0=a_sc[j % 2][:],
                        data1=u_sb[:, j * TC:(j + 1) * TC], initial=0.0,
                        op0=ALU.mult, op1=ALU.add)))(),
                      waits=[("aexp", j), ("dtmm",)],
                      inc=("v", 1))
                    E("vector", (lambda j=j: (lambda e: e.tensor_copy(
                        out=hl_sb[:, j:j + 1], in_=scr[0][:, TC - 1:TC])))(),
                      inc=("v", 1))
                    E("vector", (lambda j=j: (lambda e: e.tensor_copy(
                        out=u_sb[:, j * TC:(j + 1) * TC], in_=scr[0][:])))(),
                      inc=("v", 1), ms=("scan", j))

                def p4_exchange():
                    E("sync", lambda e: e.dma_start(out=cc_in[:, 0:16], in_=achk[:]),
                      waits=[("achk",)], inc=("cci", 16))
                    E("sync", lambda e: e.dma_start(out=cc_in[:, 16:32], in_=hl_sb[:]),
                      waits=[("scan", NE - 1)], inc=("cci", 16), ms=("ccin",))
                    E("gpsimd", lambda e: e.collective_compute(
                        "AllGather", ALU.bypass,
                        replica_groups=[list(range(NCORE))],
                        ins=[cc_in.ap().opt()], outs=[cc_res.ap().opt()]),
                      waits=[("ccin",)], inc=("cc", 1), ms=("ccgo",))
                    for r in range(NCORE):
                        E("gpsimd", (lambda r=r: (lambda e: e.dma_start(
                            out=carr[:, r * 32:(r + 1) * 32],
                            in_=cc_res[r * 128:(r + 1) * 128, :])))(),
                          waits=[("ccgo",)], inc=("car", 16))
                    ctx.ms[("carr",)] = here("car")
                    E("vector", lambda e: e.memset(hin[:], 0.0), inc=("v", 1))
                    for r in range(NCORE):
                        E("vector", (lambda r=r: (lambda e: e.tensor_tensor(
                            out=wk16a[:], in0=carr[:, r * 32:r * 32 + 16],
                            in1=msk_sb[:, r * 16:(r + 1) * 16], op=ALU.mult)))(),
                          waits=[("carr",), ("ldall",)], inc=("v", 1))
                        E("vector", (lambda r=r: (lambda e: e.tensor_tensor(
                            out=wk16a[:], in0=wk16a[:], in1=imsk_sb[:, r * 16:(r + 1) * 16],
                            op=ALU.add)))(),
                          inc=("v", 1))
                        E("vector", lambda e: e.tensor_tensor(out=hin[:], in0=hin[:], in1=wk16a[:],
                                                              op=ALU.mult),
                          inc=("v", 1))
                        E("vector", (lambda r=r: (lambda e: e.tensor_tensor(
                            out=wk16b[:], in0=carr[:, r * 32 + 16:r * 32 + 32],
                            in1=msk_sb[:, r * 16:(r + 1) * 16], op=ALU.mult)))(),
                          inc=("v", 1))
                        E("vector", lambda e: e.tensor_tensor(out=hin[:], in0=hin[:], in1=wk16b[:],
                                                              op=ALU.add),
                          inc=("v", 1))
                    ctx.ms[("hin",)] = ("v", ctx.ctr["v"])

                def p5_step(j):
                    for h2 in range(2):
                        dst = bc[j % 3][0:64, :] if h2 == 0 else bc[j % 3][64:128, :]
                        E("sync", (lambda dst=dst, j=j, h2=h2: (lambda e: e.dma_start(
                            out=dst, in_=csp_d[2 * j + h2:2 * j + h2 + 1, :].broadcast_to([64, TC]))))(),
                          waits=[("cspst",),
                                 ("pexp", j - 3) if j >= 3 else ("aexp", NE - 3 + j)],
                          inc=(f"bc{j % 3}s", 16))
                    ctx.ms[("bcp", j)] = here(f"bc{j % 3}s")
                    E("scalar", (lambda j=j: (lambda e: e.activation(
                        out=a_sc[j % 2][:], in_=bc[j % 3][:], func=AF.Exp,
                        scale=acont[:, j:j + 1])))(),
                      waits=[("bcp", j),
                             ("corr", j - 2) if j >= 2 else ("scan", NE - 2 + j)],
                      inc=("s", 1), ms=("pexp", j))
                    E("vector", (lambda j=j: (lambda e: e.scalar_tensor_tensor(
                        out=scr[0][:], in0=a_sc[j % 2][:], scalar=hin[:, j:j + 1],
                        in1=u_sb[:, j * TC:(j + 1) * TC], op0=ALU.mult, op1=ALU.add)))(),
                      waits=[("pexp", j), ("hin",), ("ysconv", j - 1) if j >= 1 else None],
                      inc=("v", 1), ms=("corr", j))
                    E("scalar", (lambda j=j: (lambda e: e.copy(
                        out=ysbf[j % 2][:], in_=scr[0][:])))(),
                      waits=[("corr", j), ("ystst", j - 2) if j >= 2 else None],
                      inc=("s", 1), ms=("ysconv", j))
                    E("sync", (lambda j=j: (lambda e: e.dma_start(
                        out=yst_d[j * 128:(j + 1) * 128, :], in_=ysbf[j % 2][:])))(),
                      waits=[("ysconv", j)], inc=(f"yb{j % 2}", 16), ms=("ystst", j))

                p3j = [0]
                p5j = [0]
                for k in range(NI + 2):
                    c = k - 2
                    if c == 6:
                        p4_exchange()
                    i = k + 2
                    if i < NI:
                        p6_load(i)
                    if 0 <= c < NI:
                        p6_compute(c)
                        if c <= 3:
                            for _ in range(4):
                                if p3j[0] < NE:
                                    p3_step(p3j[0])
                                    p3j[0] += 1
                        if c in (9, 11, 13, 15):
                            for _ in range(4):
                                if p5j[0] < NE:
                                    p5_step(p5j[0])
                                    p5j[0] += 1
                pgpu.close()

        if _PH <= 5:
            reg_tap("dbg_u", u_sb[:], [128, NE * TC], F32, ("scan", NE - 1))
            reg_tap("dbg_yst", yst_d, [D, TC], BF16, ("ystst", NE - 1))
            reg_tap("dbg_hl", hl_sb[:], [128, NE], F32, ("scan", NE - 1))
            reg_tap("dbg_achk", achk[:], [128, NE], F32, ("achk",))
            reg_tap("dbg_hin", hin[:], [128, NE], F32, ("hin",))
            reg_tap("dbg_carr", carr[:], [128, 256], F32, ("carr",))
            reg_tap("dbg_sp", sp_d, [NH, TC], F32, ("spst",))
            reg_tap("dbg_csp", csp_d, [NH, TC], F32, ("cspst",))
            reg_tap("dbg_xnt", xnT[:], [128, NE * TC], BF16, ("xnT",))
            return finish()

        b_outer = ExitStack()
        b_outer_holder[0] = b_outer
        with ExitStack() as b1:

            gu_stack.close()
            gu_holder[0] = None
            xnT_stack.close()
            xnT_holder[0] = None
            g_tm = b_outer.enter_context(nc.sbuf_tensor("g_tm", [128, NT * D], BF16, side="right"))
            ysT2 = b_outer.enter_context(nc.sbuf_tensor("ysT2", [128, NE * TC], BF16, side="right"))
            woqbR = b_outer.enter_context(nc.sbuf_tensor("woqbR", [128, NE * 512], BF16, side="right"))
            woqbL = [None, None]

            def wo_buf(ch):
                return woqbR if ch == 0 else woqbL[ch % 2]

            wo_pos = [0]

            def wo_trickle(n):
                for _ in range(n):
                    dd = wo_pos[0]
                    if dd >= NE:
                        return
                    wo_pos[0] += 1
                    E("gpsimd", (lambda dd=dd: (lambda e: e.dma_start(
                        out=woqbR[:, dd * 512:(dd + 1) * 512],
                        in_=ap["wo"][0, dd, :, :])))(),
                      waits=[("gumm", NI - 1) if dd == 0 else None],
                      inc=("wq0", 16))
                    if dd == NE - 1:
                        ctx.ms[("woqd", 0)] = here("wq0")

            def wo_load(ch):
                for dd in range(NE):
                    E("gpsimd", (lambda ch=ch, dd=dd: (lambda e: e.dma_start(
                        out=woqbL[ch % 2][:, dd * 512:(dd + 1) * 512],
                        in_=ap["wo"][ch, dd, :, :])))(),
                      waits=[("ymm", ch - 2) if (ch >= 2 and dd == 0) else None,
                             ("downmm", NE - 1) if (ch == 1 and dd == 0) else None],
                      inc=(f"wq{ch % 2}", 16))
                ctx.ms[("woqd", ch)] = here(f"wq{ch % 2}")
            for dd in range(NE):
                E("sync", (lambda dd=dd: (lambda e: e.dma_start(
                    out=ysT2[:, dd * TC:(dd + 1) * TC],
                    in_=yst_d[dd * 128:(dd + 1) * 128, :])))(),
                  waits=[("gumm", NI - 1) if dd == 0 else None,
                         ("ystst", NE - 2) if dd == 0 else None,
                         ("ystst", NE - 1) if dd == 0 else None],
                  inc=("ysl", 16))
            ctx.ms[("yst2all",)] = here("ysl")

            with ExitStack() as b2:
                dwb = [b2.enter_context(nc.sbuf_tensor(f"dwb{b}", [128, IT], BF16)) for b in range(2)]
                gt_sc = [b2.enter_context(nc.sbuf_tensor(f"gt{b}", [128, TC], BF16)) for b in range(2)]
                with nc.psum_tensor("pd0", [128, TC], F32) as pd0, \
                     nc.psum_tensor("pd1", [128, TC], F32) as pd1:
                    pd = [pd0, pd1]
                    for Et in range(NE):
                        for q in range(2):
                            E("gpsimd", (lambda Et=Et, q=q: (lambda e: e.dma_start(
                                out=dwb[Et % 2][:, q * (IT // 2):(q + 1) * (IT // 2)],
                                in_=ap["dw"][Et, :, q * (IT // 2):(q + 1) * (IT // 2)])))(),
                              waits=[("downmm", Et - 2) if Et >= 2 else None,
                                     ("ystst", NE - 2) if Et == 0 and q == 0 else None,
                                     ("ystst", NE - 1) if Et == 0 and q == 0 else None],
                              inc=(f"dw{Et % 2}", 16))
                        ctx.ms[("dwdma2", Et)] = here(f"dw{Et % 2}")
                        if Et >= 2:
                            wo_trickle(4)
                        for nh in range(2):
                            for i in range(NI):
                                first = (nh == 0 and i == 0)
                                last = (nh == 1 and i == NI - 1)
                                E("tensor", (lambda Et=Et, nh=nh, i=i: (lambda e: e.matmul(
                                    out=pd[Et % 2][:, nh * 512:(nh + 1) * 512],
                                    lhsT=dwb[Et % 2][:, i * 128:(i + 1) * 128],
                                    rhs=hdn_slice(i)[:, nh * 512: nh * 512 + 512],
                                    start=(i == 0), stop=(i == NI - 1))))(),
                                  waits=([("dwdma2", Et), ("hdn", NI - 1),
                                          ("gt", Et - 2) if Et >= 2 else None]
                                         if first else []),
                                  inc=(("t", 1) if last else None),
                                  ms=(("downmm", Et) if last else None))
                        E("scalar", (lambda Et=Et: (lambda e: e.activation(
                            out=gt_sc[Et % 2][:], in_=pd[Et % 2][:], func=AF.Silu,
                            scale=ds_sb[:, Et:Et + 1])))(),
                          waits=[("downmm", Et), ("ldall",),
                                 ("xbar", Et - 2) if Et >= 2 else None],
                          inc=("s", 1), ms=("gt", Et))
                        for tt in range(NT):
                            E("sync", (lambda Et=Et, tt=tt: (lambda e: e.dma_start_transpose(
                                out=g_tm[:, tt * D + Et * 128: tt * D + (Et + 1) * 128],
                                in_=gt_sc[Et % 2][:, tt * 128:(tt + 1) * 128])))(),
                              waits=[("gt", Et)], inc=(f"xb{Et % 2}", 16))
                        ctx.ms[("xbar", Et)] = here(f"xb{Et % 2}")

        u_stack.close()

        if _PH <= 7:
            reg_tap("dbg_hdna", hdnT_a[:], [128, NSPLIT * TC], BF16, ("hdn", NI - 1))
            reg_tap("dbg_hdnb", u_bf[:, 0:(NI - NSPLIT) * TC], [128, (NI - NSPLIT) * TC], BF16, ("hdn", NI - 1))
            reg_tap("dbg_dwb", dwb[(NE - 1) % 2][:], [128, IT], BF16, ("dwdma2", NE - 1))
            reg_tap("dbg_gtm", g_tm[:], [128, NT * D], BF16, ("xbar", NE - 1))
            reg_tap("dbg_yst", yst_d, [D, TC], BF16, ("ystst", NE - 1))
            reg_tap("dbg_hin", hin[:], [128, NE], F32, ("hin",))
            return finish()

        with ExitStack() as c1:
            woqbL[0] = c1.enter_context(nc.sbuf_tensor("woqbL0", [128, NE * 512], BF16))
            woqbL[1] = c1.enter_context(nc.sbuf_tensor("woqbL1", [128, NE * 512], BF16))
            x2c = [c1.enter_context(nc.sbuf_tensor(f"x2c{b}", [128, 512], F32)) for b in range(2)]
            sg_sc = [c1.enter_context(nc.sbuf_tensor(f"sgs{b}", [128, 512], F32)) for b in range(2)]
            outc = [c1.enter_context(nc.sbuf_tensor(f"outc{b}", [128, 512], F32)) for b in range(4)]
            with ExitStack() as cps:
                py = [cps.enter_context(nc.psum_tensor(f"py{tt}", [128, 512], F32))
                      for tt in range(NT)]
                wo_load(1)
                for ch in range(4):
                    if ch + 1 >= 2 and ch + 1 < 4:
                        wo_load(ch + 1)
                    for tt in range(NT):
                        lin = ch * NT + tt
                        for dd in range(NE):
                            first = (dd == 0)
                            last = (dd == NE - 1)
                            E("tensor", (lambda ch=ch, dd=dd, tt=tt: (lambda e: e.matmul(
                                out=py[tt][:],
                                lhsT=ysT2[:, dd * TC + tt * 128: dd * TC + (tt + 1) * 128],
                                rhs=wo_buf(ch)[:, dd * 512:(dd + 1) * 512],
                                start=(dd == 0), stop=(dd == NE - 1))))(),
                              waits=([("woqd", ch),
                                      ("yst2all",),
                                      ("gt", NE - 1) if lin == 0 else None,
                                      ("sgmul", ch - 1, tt) if ch >= 1 else None]
                                     if first else []),
                              inc=(("t", 1) if last else None),
                              ms=(("ymmT", ch, tt) if last else None))
                        E("gpsimd", (lambda ch=ch, tt=tt, lin=lin: (lambda e: e.dma_start(
                            out=x2c[lin % 2][:],
                            in_=ap["x"][tt * 128:(tt + 1) * 128, ch * 512:(ch + 1) * 512])))(),
                          waits=[("sgst", (lin - 2) // NT, (lin - 2) % NT) if lin >= 2 else
                                 ("downmm", NE - 1),
                                 ("gt", NE - 1) if lin == 0 else None],
                          inc=(f"x2{lin % 2}", 16), ms=("x2", ch, tt))
                        E("vector", (lambda ch=ch, tt=tt: (lambda e: e.tensor_tensor(
                            out=sg_sc[tt % 2][:], in0=py[tt][:],
                            in1=g_tm[:, tt * D + ch * 512: tt * D + ch * 512 + 512],
                            op=ALU.mult)))(),
                          waits=[("ymmT", ch, tt),
                                 ("xbar", NE - 2), ("xbar", NE - 1),
                                 ("sgst", (lin - 2) // NT, (lin - 2) % NT) if lin >= 2 else None],
                          inc=("v", 1), ms=("sgmul", ch, tt))
                        E("vector", (lambda ch=ch, tt=tt, lin=lin: (lambda e: e.tensor_tensor(
                            out=outc[tt % 4][:], in0=sg_sc[tt % 2][:], in1=x2c[lin % 2][:],
                            op=ALU.add)))(),
                          waits=[("x2", ch, tt),
                                 ("outst", (lin - 4) // NT, (lin - 4) % NT) if lin >= 4 else None],
                          inc=("v", 1), ms=("sgst", ch, tt))
                        E("sync", (lambda ch=ch, tt=tt: (lambda e: e.dma_start(
                            out=ap["out"][tt * 128:(tt + 1) * 128, ch * 512:(ch + 1) * 512],
                            in_=outc[tt % 4][:])))(),
                          waits=[("sgst", ch, tt)], inc=(f"os{tt % 4}", 16), ms=("outst", ch, tt))
                    ctx.ms[("ymm", ch)] = ctx.ms[("ymmT", ch, NT - 1)]

        if DEBUG:
            reg_tap("dbg_yst", yst_d, [D, TC], BF16, ("ystst", NE - 1))
            reg_tap("dbg_gtm", g_tm[:], [128, NT * D], BF16, ("xbar", NE - 1))
            reg_tap("dbg_hin", hin[:], [128, NE], F32, ("hin",))
            reg_tap("dbg_achk", achk[:], [128, NE], F32, ("achk",))
            reg_tap("dbg_hl", hl_sb[:], [128, NE], F32, ("scan", NE - 1))
            reg_tap("dbg_carr", carr[:], [128, 256], F32, ("carr",))
            reg_tap("dbg_sp", sp_d, [NH, TC], F32, ("spst",))
            reg_tap("dbg_csp", csp_d, [NH, TC], F32, ("cspst",))
        return finish()


def _prep_inputs(kw):
    f = np.ascontiguousarray
    W_in = np.asarray(kw["L0_attn_W_in"], np.float32)
    W_out = np.asarray(kw["L0_attn_W_out"], np.float32)
    log_A = np.asarray(kw["L0_attn_log_A"], np.float32)
    W_dt = np.asarray(kw["L0_attn_W_dt"], np.float32)
    b_dt = np.asarray(kw["L0_attn_b_dt"], np.float32)
    gate_w = np.asarray(kw["L0_ffn_gate_w"], np.float32)
    up_w = np.asarray(kw["L0_ffn_up_w"], np.float32)
    down_w = np.asarray(kw["L0_ffn_down_w"], np.float32)
    gate_s = np.asarray(kw["L0_ffn_gate_s"], np.float32)
    up_s = np.asarray(kw["L0_ffn_up_s"], np.float32)
    down_s = np.asarray(kw["L0_ffn_down_s"], np.float32)

    com = {
        "w1": f(W_in.reshape(NE, 128, NE, 128).transpose(0, 3, 2, 1).reshape(NE, 128, D)),
        "wdt": f(W_dt.T.reshape(NE, 128, NH)),
        "bdt": f(b_dt.reshape(NH, 1)),
        "alog": f(log_A.reshape(NE, 128).T),
        "gw": f(gate_w.reshape(NI, 128, NE, 128).transpose(0, 3, 2, 1).reshape(NI, 128, D)),
        "uw": f(up_w.reshape(NI, 128, NE, 128).transpose(0, 3, 2, 1).reshape(NI, 128, D)),
        "dw": f(down_w.reshape(NE, 128, NI, 128).transpose(0, 3, 2, 1).reshape(NE, 128, IT)),
        "wo": f(W_out.reshape(4, 512, NE, 128).transpose(0, 2, 3, 1)),
        "gsc": f(gate_s.reshape(NI, 128).T),
        "usc": f(up_s.reshape(NI, 128).T),
        "dsc": f(down_s.reshape(NE, 128).T),
        "ident": np.eye(128, dtype=np.float32),
    }
    x_full = np.asarray(kw["L0_attn_x"], np.float32).reshape(NCORE * TC, D)
    in_maps = []
    for r in range(NCORE):
        m = np.zeros((128, 128), np.float32)
        for j in range(NCORE):
            if j < r:
                m[:, j * 16:(j + 1) * 16] = 1.0
        im = 1.0 - m
        d = dict(com)
        d["x"] = f(x_full[r * TC:(r + 1) * TC, :])
        d["msk"] = m
        d["imsk"] = im
        in_maps.append(d)
    return in_maps


_CACHE = {}
DEBUG = False
_PH = 8


def kernel(**inputs):
    in_maps = _prep_inputs(inputs)
    key = ("nc", DEBUG, _PH)
    if key not in _CACHE:
        _, ctx1 = build(dry=True)
        _CACHE[key] = build(dry=False, M=ctx1.ms)[0]
    nc = _CACHE[key]
    res = run_bass_kernel_spmd(nc, in_maps, core_ids=list(range(NCORE)))
    _CACHE["last_results"] = res.results
    out = np.concatenate([r["out"] for r in res.results], axis=0)
    return out.reshape(1, NCORE * TC, D).astype(np.float32)


# version 17
# speedup vs baseline: 1.0038x; 1.0038x over previous
import numpy as np
import ml_dtypes
from contextlib import ExitStack

import concourse.bass as bass
import concourse.mybir as mybir
from concourse.bass_utils import run_bass_kernel_spmd

F32 = mybir.dt.float32
BF16 = mybir.dt.bfloat16
AF = mybir.ActivationFunctionType
ALU = mybir.AluOpType
AX = mybir.AxisListType

NCORE = 8
TC = 1024
D = 2048
NE = 16
NT = 8
NI = 44
NH = 32
IT = 5632
DWQ = 1408
NSPLIT = 12

ENGS = ("sync", "gpsimd", "scalar", "vector", "tensor")
OWN_SEM = {"vector": "v", "scalar": "s"}


class Ctx:
    def __init__(self, dry, M=None):
        self.dry = dry
        self.M = M or {}
        self.ctr = {}
        self.ms = {}
        self.eng_ops = {e: [] for e in ENGS}

    def emit(self, eng, fn=None, waits=(), inc=None, ms=None):
        if inc is None and eng in OWN_SEM:
            inc = (OWN_SEM[eng], 1)
        if inc is not None:
            sem, amt = inc
            self.ctr[sem] = self.ctr.get(sem, 0) + amt
            if ms is not None:
                self.ms[ms] = (sem, self.ctr[sem])
        if not self.dry:
            rw = []
            for k in waits:
                if k is None:
                    continue
                if k not in self.M:
                    raise KeyError(f"milestone {k} not in pass-1 table")
                rw.append(self.M[k])
            self.eng_ops[eng].append((fn, tuple(rw), inc))

    def w(self, key):
        return key


def _body(engname, ops, sems):
    own = OWN_SEM.get(engname)

    def body(e):
        waited = {}
        own_count = 0
        for fn, waits, inc in ops:
            if own is not None and own_count > 0:
                if own_count > waited.get(own, 0):
                    e.wait_ge(sems[own], own_count)
                    waited[own] = own_count
            for semname, val in waits:
                if val <= waited.get(semname, 0):
                    continue
                e.wait_ge(sems[semname], val)
                waited[semname] = val
            instr = fn(e)
            if inc is not None:
                instr.then_inc(sems[inc[0]], inc[1])
                if own is not None and inc[0] == own:
                    own_count += inc[1]
    return body


def build(dry, M=None):
    nc = bass.Bass("TRN2", target_bir_lowering=False)
    ctx = Ctx(dry, M)

    ap = {}
    ap["x"] = nc.dram_tensor("x", [TC, D], F32, kind="ExternalInput").ap()
    ap["w1"] = nc.dram_tensor("w1", [NE, 128, D], F32, kind="ExternalInput").ap()
    ap["wdt"] = nc.dram_tensor("wdt", [NE, 128, NH], F32, kind="ExternalInput").ap()
    ap["bdt"] = nc.dram_tensor("bdt", [NH, 1], F32, kind="ExternalInput").ap()
    ap["alog"] = nc.dram_tensor("alog", [128, NE], F32, kind="ExternalInput").ap()
    ap["gw"] = nc.dram_tensor("gw", [NI, 128, D], F32, kind="ExternalInput").ap()
    ap["uw"] = nc.dram_tensor("uw", [NI, 128, D], F32, kind="ExternalInput").ap()
    ap["dw"] = nc.dram_tensor("dw", [NE, 128, IT], F32, kind="ExternalInput").ap()
    ap["wo"] = nc.dram_tensor("wo", [4, NE, 128, 512], F32, kind="ExternalInput").ap()
    ap["gsc"] = nc.dram_tensor("gsc", [128, NI], F32, kind="ExternalInput").ap()
    ap["usc"] = nc.dram_tensor("usc", [128, NI], F32, kind="ExternalInput").ap()
    ap["dsc"] = nc.dram_tensor("dsc", [128, NE], F32, kind="ExternalInput").ap()
    ap["msk"] = nc.dram_tensor("msk", [128, 128], F32, kind="ExternalInput").ap()
    ap["imsk"] = nc.dram_tensor("imsk", [128, 128], F32, kind="ExternalInput").ap()
    ap["ident"] = nc.dram_tensor("ident", [128, 128], F32, kind="ExternalInput").ap()
    ap["out"] = nc.dram_tensor("out", [TC, D], F32, kind="ExternalOutput").ap()
    sp_d = nc.dram_tensor("sp_d", [NH, TC], F32).ap()
    csp_d = nc.dram_tensor("csp_d", [NH, TC], F32).ap()
    yst_d = nc.dram_tensor("yst_d", [D, TC], BF16).ap()
    cc_in = nc.dram_tensor("cc_in", [128, 32], F32)
    cc_res = nc.dram_tensor("cc_res", [NCORE * 128, 32], F32)

    E = ctx.emit

    with ExitStack() as outer:
        def sb(name, shape, dt, stack=outer):
            return stack.enter_context(nc.sbuf_tensor(name, shape, dt))

        b_outer_holder = [None]
        xnT_holder = [None]
        gu_holder = [None]
        xnT_stack = ExitStack()
        xnT_holder[0] = xnT_stack
        xnT = xnT_stack.enter_context(nc.sbuf_tensor("xnT", [128, NE * TC], BF16, side="right"))
        hdnT_a = outer.enter_context(nc.sbuf_tensor("hdnT_a", [128, NSPLIT * TC], BF16))

        identf = sb("identf", [128, 128], F32)
        identb = sb("identb", [128, 128], BF16)
        msk_sb = sb("msk_sb", [128, 128], F32)
        imsk_sb = sb("imsk_sb", [128, 128], F32)
        alog_sb = sb("alog_sb", [128, NE], F32)
        aexp_sb = sb("aexp_sb", [128, NE], F32)
        acont = sb("acont", [128, NE], F32)
        gs_sb = sb("gs_sb", [128, NI], F32)
        us_sb = sb("us_sb", [128, NI], F32)
        ds_sb = sb("ds_sb", [128, NE], F32)
        bdt_sb = sb("bdt_sb", [NH, 1], F32)
        wdt_sb = sb("wdt_sb", [128, NE * NH], F32)
        hl_sb = sb("hl_sb", [128, NE], F32)
        alast = sb("alast", [128, NE], F32)
        achkp = sb("achkp", [128, NE], F32)
        achk = sb("achk", [128, NE], F32)
        hin = sb("hin", [128, NE], F32)
        wk16a = sb("wk16a", [128, NE], F32)
        wk16b = sb("wk16b", [128, NE], F32)
        carr = sb("carr", [128, 256], F32)
        red = sb("red", [128, NT], F32)
        negm = sb("negm", [128, NT], F32)
        ssq = sb("ssq", [128, NT], F32)
        veps = sb("veps", [128, NT], F32)
        lnv = sb("lnv", [128, NT], F32)
        rstd = sb("rstd", [128, NT], F32)
        gu_stack = ExitStack()
        gu_holder[0] = gu_stack
        gwb = [gu_stack.enter_context(nc.sbuf_tensor(f"gwb{b}", [128, D], BF16, side="right")) for b in range(3)]
        uwb = [gu_stack.enter_context(nc.sbuf_tensor(f"uwb{b}", [128, D], BF16, side="right")) for b in range(3)]
        w1b = [gu_stack.enter_context(nc.sbuf_tensor(f"w1b{b}", [128, D], BF16, side="right")) for b in range(2)]
        sgb = [gu_stack.enter_context(nc.sbuf_tensor(f"sgb{b}", [128, TC], BF16, side="right")) for b in range(2)]
        uub = [gu_stack.enter_context(nc.sbuf_tensor(f"uub{b}", [128, TC], BF16, side="right")) for b in range(2)]

        SEMNAMES = ("sml", "xt0s", "xt1s", "xt2s", "xt3s", "xt4s", "xt5s", "xt6s", "xt7s", "w1s", "sps", "als", "bc0s", "bc1s",
                    "sm2", "bc2s", "cci", "cc", "car", "yb0", "yb1", "gw0", "gw1", "gw2", "uw0", "uw1", "uw2",
                    "dw0", "dw1", "xb0", "xb1", "ysl", "wq0", "wq1", "x20", "x21",
                    "os0", "os1", "os2", "os3", "dbs", "t", "v", "s")
        sems = {n: outer.enter_context(nc.semaphore(n)) for n in SEMNAMES}

        def here(sem):
            return (sem, ctx.ctr.get(sem, 0))

        def p6_load(i):
            E("gpsimd", (lambda i=i: (lambda e: e.dma_start(
                out=gwb[i % 3][:], in_=ap["gw"][i, :, :])))(),
              waits=[("gumm", i - 3) if i >= 3 else None],
              inc=(f"gw{i % 3}", 16), ms=("gdmah", i, 0))
            E("gpsimd", (lambda i=i: (lambda e: e.dma_start(
                out=uwb[i % 3][:], in_=ap["uw"][i, :, :])))(),
              waits=[("gumm", i - 3) if i >= 3 else None],
              inc=(f"uw{i % 3}", 16), ms=("udmah", i, 0))

        taps = []

        def reg_tap(name, src, shape, dt_, wait):
            taps.append((name, src, shape, dt_, wait))

        def finish():
            for name, src, shape, dt_, wait in taps:
                dout = nc.dram_tensor(name, shape, dt_, kind="ExternalOutput").ap()
                E("gpsimd", (lambda dout=dout, src=src: (lambda e: e.dma_start(
                    out=dout, in_=src)))(),
                  waits=[wait], inc=("dbs", 16))
            finals = []
            for sname in SEMNAMES:
                if sname in ("t", "v", "s", "cc"):
                    continue
                tot = ctx.ctr.get(sname, 0)
                if tot:
                    ctx.ms[("fin", sname)] = (sname, tot)
                    finals.append(("fin", sname))
            if finals:
                E("sync", lambda e: e.nop(), waits=finals)
                E("gpsimd", lambda e: e.nop(), waits=finals)
            if not dry:
                with nc.Block() as block:
                    block.sync(_body("sync", ctx.eng_ops["sync"], sems))
                    block.gpsimd(_body("gpsimd", ctx.eng_ops["gpsimd"], sems))
                    block.scalar(_body("scalar", ctx.eng_ops["scalar"], sems))
                    block.vector(_body("vector", ctx.eng_ops["vector"], sems))
                    block.tensor(_body("tensor", ctx.eng_ops["tensor"], sems))
            for st_ in (b_outer_holder, xnT_holder, gu_holder):
                if st_ and st_[0] is not None:
                    st_[0].close()
                    st_[0] = None
            return nc, ctx

        E("sync", lambda e: e.dma_start(out=identf[:], in_=ap["ident"]),
          inc=("sml", 16))
        E("sync", lambda e: e.dma_start(out=alog_sb[:], in_=ap["alog"]),
          inc=("sml", 16), ms=("identld",))

        def small_loads():
            small = [("msk", msk_sb[:], ap["msk"]),
                     ("imsk", imsk_sb[:], ap["imsk"]),
                     ("gs", gs_sb[:], ap["gsc"]), ("us", us_sb[:], ap["usc"]),
                     ("ds", ds_sb[:], ap["dsc"]), ("bdt", bdt_sb[:], ap["bdt"])]
            for name, dst, srcap in small:
                E("gpsimd", (lambda d=dst, s=srcap: (lambda e: e.dma_start(out=d, in_=s)))(),
                  inc=("sm2", 16))
            for Ew in range(NE):
                E("gpsimd", (lambda Ew=Ew: (lambda e: e.dma_start(
                    out=wdt_sb[:, Ew * NH:(Ew + 1) * NH], in_=ap["wdt"][Ew, :, :])))(),
                  inc=("sm2", 16))
            ctx.ms[("ldall",)] = here("sm2")

        E("scalar", lambda e: e.copy(out=identb[:], in_=identf[:]),
          waits=[("identld",)], inc=("s", 1), ms=("identb",))
        E("scalar", lambda e: e.activation(out=aexp_sb[:], in_=alog_sb[:], func=AF.Exp),
          waits=[("identld",)], inc=("s", 1), ms=("aexpc",))
        E("vector", lambda e: e.tensor_scalar(out=acont[:], in0=aexp_sb[:], scalar1=-1.0,
                                              scalar2=None, op0=ALU.mult),
          waits=[("aexpc",)], inc=("v", 1), ms=("acont",))

        with ExitStack() as a1:
            xt = [a1.enter_context(nc.sbuf_tensor(f"xt{b}", [128, D], F32)) for b in range(NT)]
            sq = a1.enter_context(nc.sbuf_tensor("sq", [128, D], F32))
            av = a1.enter_context(nc.sbuf_tensor("av", [128, NT], F32))
            m2 = a1.enter_context(nc.sbuf_tensor("m2", [128, NT], F32))
            xnb = [a1.enter_context(nc.sbuf_tensor(f"xnb{b}", [128, D], BF16)) for b in range(2)]
            with nc.psum_tensor("ps_tr0", [128, 128], BF16) as pt0, \
                 nc.psum_tensor("ps_tr1", [128, 128], BF16) as pt1:
                ps_tr = [pt0, pt1]
                for tt in range(NT):
                    E("sync", (lambda tt=tt: (lambda e: e.dma_start(
                        out=xt[tt][:], in_=ap["x"][tt * 128:(tt + 1) * 128, :])))(),
                      inc=(f"xt{tt}s", 16), ms=("xload", tt))
                    E("vector", (lambda tt=tt: (lambda e: e.reduce_sum(
                        out=red[:, tt:tt + 1], in_=xt[tt][:], axis=AX.X)))(),
                      waits=[("xload", tt)], inc=("v", 1), ms=("red", tt))
                    E("scalar", (lambda tt=tt: (lambda e: e.activation(
                        out=sq[:], in_=xt[tt][:], func=AF.Square,
                        accum_out=ssq[:, tt:tt + 1])))(),
                      waits=[("xload", tt)], inc=("s", 1), ms=("ssq", tt))
                for tt in range(NT):
                    E("vector", (lambda tt=tt: (lambda e: e.tensor_scalar(
                        out=negm[:, tt:tt + 1], in0=red[:, tt:tt + 1],
                        scalar1=-1.0 / D, scalar2=None, op0=ALU.mult)))(),
                      waits=[("red", tt)], inc=("v", 1), ms=("negm", tt))
                    E("vector", (lambda tt=tt: (lambda e: e.tensor_scalar(
                        out=av[:, tt:tt + 1], in0=ssq[:, tt:tt + 1],
                        scalar1=1.0 / D, scalar2=1e-5, op0=ALU.mult, op1=ALU.add)))(),
                      waits=[("ssq", tt)], inc=("v", 1))
                    E("vector", (lambda tt=tt: (lambda e: e.tensor_tensor(
                        out=m2[:, tt:tt + 1], in0=negm[:, tt:tt + 1],
                        in1=negm[:, tt:tt + 1], op=ALU.mult)))(),
                      inc=("v", 1))
                    E("vector", (lambda tt=tt: (lambda e: e.tensor_sub(
                        out=veps[:, tt:tt + 1], in0=av[:, tt:tt + 1],
                        in1=m2[:, tt:tt + 1])))(),
                      inc=("v", 1), ms=("veps", tt))
                    E("scalar", (lambda tt=tt: (lambda e: e.activation(
                        out=lnv[:, tt:tt + 1], in_=veps[:, tt:tt + 1], func=AF.Ln)))(),
                      waits=[("veps", tt)], inc=("s", 1))
                    E("scalar", (lambda tt=tt: (lambda e: e.activation(
                        out=rstd[:, tt:tt + 1], in_=lnv[:, tt:tt + 1], func=AF.Exp,
                        scale=-0.5)))(),
                      inc=("s", 1), ms=("rstd", tt))
                for tt in range(NT):
                    E("vector", (lambda tt=tt: (lambda e: e.tensor_scalar(
                        out=xnb[tt % 2][:], in0=xt[tt][:], scalar1=negm[:, tt:tt + 1],
                        scalar2=rstd[:, tt:tt + 1], op0=ALU.add, op1=ALU.mult)))(),
                      waits=[("rstd", tt), ("trdone", tt - 2) if tt >= 2 else None],
                      inc=("v", 1), ms=("xnb", tt))
                    for d in range(NE):
                        k = (tt * NE + d) % 2
                        E("tensor", (lambda tt=tt, d=d, k=k: (lambda e: e.transpose(
                            out=ps_tr[k][:], in_=xnb[tt % 2][:, d * 128:(d + 1) * 128],
                            identity=identb[:])))(),
                          waits=[("xnb", tt), ("identb",), ("trcopy", tt, d - 2) if d >= 2
                                 else (("trcopy", tt - 1, NE - 2 + d) if tt >= 1 else None)],
                          inc=("t", 1), ms=("tr", tt, d))
                        E("vector", (lambda tt=tt, d=d, k=k: (lambda e: e.tensor_copy(
                            out=xnT[:, d * TC + tt * 128: d * TC + (tt + 1) * 128],
                            in_=ps_tr[k][:])))(),
                          waits=[("tr", tt, d)], inc=("v", 1), ms=("trcopy", tt, d))
                    ctx.ms[("trdone", tt)] = ctx.ms[("trcopy", tt, NE - 1)]
            ctx.ms[("xnT",)] = ctx.ms[("trcopy", NT - 1, NE - 1)]

        if _PH <= 1:
            reg_tap("dbg_red", red[:], [128, NT], F32, ("xnT",))
            reg_tap("dbg_negm", negm[:], [128, NT], F32, ("xnT",))
            reg_tap("dbg_ssq", ssq[:], [128, NT], F32, ("xnT",))
            reg_tap("dbg_veps", veps[:], [128, NT], F32, ("xnT",))
            reg_tap("dbg_rstd", rstd[:], [128, NT], F32, ("xnT",))
            reg_tap("dbg_xnt", xnT[:], [128, NE * TC], BF16, ("xnT",))
            return finish()

        u_stack = ExitStack()
        with ExitStack() as a2:
            u_sb = u_stack.enter_context(nc.sbuf_tensor("u_sb", [128, NE * TC], F32))
            u_bf = u_sb.bitcast(BF16)
            with ExitStack() as a2w:
                zch = [a2w.enter_context(nc.sbuf_tensor(f"zch{i}", [NH, TC], F32)) for i in range(4)]
                z_sb, zs1, zs2, zs3 = zch
                csp_sb = a2w.enter_context(nc.sbuf_tensor("csp_sb", [NH, TC], F32))

                with nc.psum_tensor("pmm0", [128, TC], F32) as pm0, \
                     nc.psum_tensor("pmm1", [128, TC], F32) as pm1, \
                     nc.psum_tensor("pdt", [NH, TC], F32) as pdt:
                    pmm = [pm0, pm1]
                    p6_load(0)
                    p6_load(1)
                    for Et in range(NE):
                        if Et == 2:
                            small_loads()
                        E("gpsimd", (lambda Et=Et: (lambda e: e.dma_start(
                            out=w1b[Et % 2][:], in_=ap["w1"][Et, :, :])))(),
                          waits=[("mm1", Et - 2) if Et >= 2 else None],
                          inc=(f"gw{Et % 2}", 16), ms=("w1dma", Et))
                        for nh in range(2):
                            for dd in range(NE):
                                first = (nh == 0 and dd == 0)
                                last = (nh == 1 and dd == NE - 1)
                                E("tensor", (lambda Et=Et, nh=nh, dd=dd: (lambda e: e.matmul(
                                    out=pmm[Et % 2][:, nh * 512:(nh + 1) * 512],
                                    lhsT=w1b[Et % 2][:, dd * 128:(dd + 1) * 128],
                                    rhs=xnT[:, dd * TC + nh * 512: dd * TC + nh * 512 + 512],
                                    start=(dd == 0), stop=(dd == NE - 1))))(),
                                  waits=([("w1dma", Et), ("xnT",),
                                          ("uev", Et - 2) if Et >= 2 else None]
                                         if first else []),
                                  inc=(("t", 1) if last else None),
                                  ms=(("mm1", Et) if last else None))
                        E("vector", (lambda Et=Et: (lambda e: e.tensor_copy(
                            out=u_sb[:, Et * TC:(Et + 1) * TC], in_=pmm[Et % 2][:])))(),
                          waits=[("mm1", Et)], inc=("v", 1), ms=("uev", Et))
                    for nh in range(2):
                        for Et in range(NE):
                            first = (nh == 0 and Et == 0)
                            last = (nh == 1 and Et == NE - 1)
                            E("tensor", (lambda Et=Et, nh=nh: (lambda e: e.matmul(
                                out=pdt[:, nh * 512:(nh + 1) * 512],
                                lhsT=wdt_sb[:, Et * NH:(Et + 1) * NH],
                                rhs=u_sb[:, Et * TC + nh * 512: Et * TC + nh * 512 + 512],
                                start=(Et == 0), stop=(Et == NE - 1))))(),
                              waits=([("uev", NE - 1), ("ldall",)] if first else []),
                              inc=(("t", 1) if last else None),
                              ms=(("dtmm",) if last else None))
                    E("vector", lambda e: e.tensor_scalar(out=z_sb[:], in0=pdt[:],
                                                          scalar1=bdt_sb[:, 0:1],
                                                          scalar2=None, op0=ALU.add),
                      waits=[("dtmm",), ("ldall",)], inc=("v", 1), ms=("z",))
                E("vector", lambda e: e.tensor_scalar(out=zs1[:], in0=z_sb[:], scalar1=-1.0,
                                                      scalar2=None, op0=ALU.mult),
                  waits=[("z",)], inc=("v", 1))
                E("vector", lambda e: e.tensor_tensor(out=zs2[:], in0=z_sb[:], in1=zs1[:],
                                                      op=ALU.min),
                  inc=("v", 1))
                E("vector", lambda e: e.tensor_scalar(out=zs1[:], in0=z_sb[:], scalar1=0.0,
                                                      scalar2=None, op0=ALU.max),
                  inc=("v", 1), ms=("zp",))
                E("scalar", lambda e: e.activation(out=zs3[:], in_=zs2[:], func=AF.Exp),
                  waits=[("zp",)], inc=("s", 1))
                E("scalar", lambda e: e.activation(out=zs2[:], in_=zs3[:], func=AF.Ln, bias=1.0),
                  inc=("s", 1), ms=("l1",))
                E("vector", lambda e: e.tensor_tensor(out=z_sb[:], in0=zs1[:], in1=zs2[:],
                                                      op=ALU.add),
                  waits=[("l1",)], inc=("v", 1), ms=("sp",))
                E("vector", lambda e: e.memset(zs3[:], 0.0), inc=("v", 1))
                E("vector", lambda e: e.tensor_tensor_scan(out=csp_sb[:], data0=z_sb[:],
                                                           data1=zs3[:], initial=0.0,
                                                           op0=ALU.add, op1=ALU.add),
                  inc=("v", 1), ms=("csp",))
                E("sync", lambda e: e.dma_start(out=sp_d, in_=z_sb[:]),
                  waits=[("sp",)], inc=("sps", 16))
                E("sync", lambda e: e.dma_start(out=csp_d, in_=csp_sb[:]),
                  waits=[("csp",)], inc=("sps", 16))
                ctx.ms[("spst",)] = ctx.ms[("cspst",)] = here("sps")
                for j in range(NE):
                    for h2 in range(2):
                        hidx = 2 * j + h2
                        dst = alast[0:64, j:j + 1] if h2 == 0 else alast[64:128, j:j + 1]
                        E("gpsimd", (lambda dst=dst, hidx=hidx: (lambda e: e.dma_start(
                            out=dst, in_=csp_d[hidx:hidx + 1, TC - 1:TC].broadcast_to([64, 1]))))(),
                          waits=[("cspst",)], inc=("als", 16))
                ctx.ms[("alast",)] = here("als")
                E("vector", lambda e: e.tensor_tensor(out=achkp[:], in0=acont[:], in1=alast[:],
                                                      op=ALU.mult),
                  waits=[("alast",), ("acont",)], inc=("v", 1), ms=("achkp",))
                E("scalar", lambda e: e.activation(out=achk[:], in_=achkp[:], func=AF.Exp),
                  waits=[("achkp",)], inc=("s", 1), ms=("achk",))

            pgpu = ExitStack()
            pg = [pgpu.enter_context(nc.psum_tensor(f"pg{b}", [128, TC], F32)) for b in range(2)]
            pu = [pgpu.enter_context(nc.psum_tensor(f"pu{b}", [128, TC], F32)) for b in range(2)]

            def hdn_slice(i):
                if i < NSPLIT:
                    return hdnT_a[:, i * TC:(i + 1) * TC]
                m = i - NSPLIT
                return u_bf[:, m * TC:(m + 1) * TC]

            def p6_compute(i):
                for which, psum, wsb in (("g", pg, gwb), ("u", pu, uwb)):
                    for nh in range(2):
                        for hh in range(NE):
                            first = (which == "g" and nh == 0 and hh == 0)
                            firstu = (which == "u" and nh == 0 and hh == 0)
                            E("tensor", (lambda i=i, which=which, nh=nh, hh=hh, psum=psum, wsb=wsb:
                                         (lambda e: e.matmul(
                                out=psum[i % 2][:, nh * 512:(nh + 1) * 512],
                                lhsT=wsb[i % 3][:, hh * 128:(hh + 1) * 128],
                                rhs=xnT[:, hh * TC + nh * 512: hh * TC + nh * 512 + 512],
                                start=(hh == 0), stop=(hh == NE - 1))))(),
                              waits=([("gdmah", i, 0), ("udmah", i, 0),
                                      ("silu", i - 2) if i >= 2 else None,
                                      ("uu", i - 2) if i >= 2 else None,
                                      ("uev", NE - 1) if i == 0 else None]
                                     if first else
                                     ([("z",)] if (firstu and i == 0) else [])),
                              inc=(("t", 1) if (nh == 1 and hh == NE - 1) else None),
                              ms=((("gmm", i) if which == "g" else ("gumm", i))
                                  if (nh == 1 and hh == NE - 1) else None))
                E("scalar", (lambda i=i: (lambda e: e.activation(
                    out=sgb[i % 2][:], in_=pg[i % 2][:], func=AF.Silu,
                    scale=gs_sb[:, i:i + 1])))(),
                  waits=[("gmm", i), ("ldall",), ("hdn", i - 2) if i >= 2 else None],
                  inc=("s", 1), ms=("silu", i))
                E("vector", (lambda i=i: (lambda e: e.tensor_scalar(
                    out=uub[i % 2][:], in0=pu[i % 2][:], scalar1=us_sb[:, i:i + 1],
                    scalar2=None, op0=ALU.mult)))(),
                  waits=[("gumm", i), ("ldall",), ("hdn", i - 2) if i >= 2 else None],
                  inc=("v", 1), ms=("uu", i))
                E("vector", (lambda i=i: (lambda e: e.tensor_tensor(
                    out=hdn_slice(i), in0=sgb[i % 2][:], in1=uub[i % 2][:],
                    op=ALU.mult)))(),
                  waits=[("silu", i),
                         ("corr", (i - NSPLIT) // 2) if i >= NSPLIT else None],
                  inc=("v", 1), ms=("hdn", i))

            if _PH <= 2:
                reg_tap("dbg_u", u_sb[:], [128, NE * TC], F32, ("uev", NE - 1))
                reg_tap("dbg_z", z_sb[:], [NH, TC], F32, ("z",))
                reg_tap("dbg_sp", sp_d, [NH, TC], F32, ("spst",))
                reg_tap("dbg_csp", csp_d, [NH, TC], F32, ("cspst",))
                reg_tap("dbg_xnt", xnT[:], [128, NE * TC], BF16, ("xnT",))
                return finish()

            with ExitStack() as a3:
                bc = [a3.enter_context(nc.sbuf_tensor(f"bc{b}", [128, TC], F32)) for b in range(3)]
                a_sc = [a3.enter_context(nc.sbuf_tensor(f"a_sc{b}", [128, TC], F32)) for b in range(2)]
                scr = [a3.enter_context(nc.sbuf_tensor("scr0", [128, TC], F32))]
                ysbf = [a3.enter_context(nc.sbuf_tensor(f"ysbf{b}", [128, TC], BF16)) for b in range(2)]

                def p3_step(j):
                    for h2 in range(2):
                        dst = bc[j % 3][0:64, :] if h2 == 0 else bc[j % 3][64:128, :]
                        E("sync", (lambda dst=dst, j=j, h2=h2: (lambda e: e.dma_start(
                            out=dst, in_=sp_d[2 * j + h2:2 * j + h2 + 1, :].broadcast_to([64, TC]))))(),
                          waits=[("spst",), ("aexp", j - 3) if j >= 3 else None],
                          inc=(f"bc{j % 3}s", 16))
                    ctx.ms[("bca", j)] = here(f"bc{j % 3}s")
                    E("scalar", (lambda j=j: (lambda e: e.activation(
                        out=a_sc[j % 2][:], in_=bc[j % 3][:], func=AF.Exp,
                        scale=acont[:, j:j + 1])))(),
                      waits=[("bca", j), ("acont",), ("scan", j - 2) if j >= 2 else None],
                      inc=("s", 1), ms=("aexp", j))
                    E("vector", (lambda j=j: (lambda e: e.tensor_tensor_scan(
                        out=scr[0][:], data0=a_sc[j % 2][:],
                        data1=u_sb[:, j * TC:(j + 1) * TC], initial=0.0,
                        op0=ALU.mult, op1=ALU.add)))(),
                      waits=[("aexp", j), ("dtmm",)],
                      inc=("v", 1))
                    E("vector", (lambda j=j: (lambda e: e.tensor_copy(
                        out=hl_sb[:, j:j + 1], in_=scr[0][:, TC - 1:TC])))(),
                      inc=("v", 1))
                    E("vector", (lambda j=j: (lambda e: e.tensor_copy(
                        out=u_sb[:, j * TC:(j + 1) * TC], in_=scr[0][:])))(),
                      inc=("v", 1), ms=("scan", j))

                def p4_exchange():
                    E("sync", lambda e: e.dma_start(out=cc_in[:, 0:16], in_=achk[:]),
                      waits=[("achk",)], inc=("cci", 16))
                    E("sync", lambda e: e.dma_start(out=cc_in[:, 16:32], in_=hl_sb[:]),
                      waits=[("scan", NE - 1)], inc=("cci", 16), ms=("ccin",))
                    E("gpsimd", lambda e: e.collective_compute(
                        "AllGather", ALU.bypass,
                        replica_groups=[list(range(NCORE))],
                        ins=[cc_in.ap().opt()], outs=[cc_res.ap().opt()]),
                      waits=[("ccin",)], inc=("cc", 1), ms=("ccgo",))
                    for r in range(NCORE):
                        E("gpsimd", (lambda r=r: (lambda e: e.dma_start(
                            out=carr[:, r * 32:(r + 1) * 32],
                            in_=cc_res[r * 128:(r + 1) * 128, :])))(),
                          waits=[("ccgo",)], inc=("car", 16))
                    ctx.ms[("carr",)] = here("car")
                    E("vector", lambda e: e.memset(hin[:], 0.0), inc=("v", 1))
                    for r in range(NCORE):
                        E("vector", (lambda r=r: (lambda e: e.tensor_tensor(
                            out=wk16a[:], in0=carr[:, r * 32:r * 32 + 16],
                            in1=msk_sb[:, r * 16:(r + 1) * 16], op=ALU.mult)))(),
                          waits=[("carr",), ("ldall",)], inc=("v", 1))
                        E("vector", (lambda r=r: (lambda e: e.tensor_tensor(
                            out=wk16a[:], in0=wk16a[:], in1=imsk_sb[:, r * 16:(r + 1) * 16],
                            op=ALU.add)))(),
                          inc=("v", 1))
                        E("vector", lambda e: e.tensor_tensor(out=hin[:], in0=hin[:], in1=wk16a[:],
                                                              op=ALU.mult),
                          inc=("v", 1))
                        E("vector", (lambda r=r: (lambda e: e.tensor_tensor(
                            out=wk16b[:], in0=carr[:, r * 32 + 16:r * 32 + 32],
                            in1=msk_sb[:, r * 16:(r + 1) * 16], op=ALU.mult)))(),
                          inc=("v", 1))
                        E("vector", lambda e: e.tensor_tensor(out=hin[:], in0=hin[:], in1=wk16b[:],
                                                              op=ALU.add),
                          inc=("v", 1))
                    ctx.ms[("hin",)] = ("v", ctx.ctr["v"])

                def p5_step(j):
                    for h2 in range(2):
                        dst = bc[j % 3][0:64, :] if h2 == 0 else bc[j % 3][64:128, :]
                        E("sync", (lambda dst=dst, j=j, h2=h2: (lambda e: e.dma_start(
                            out=dst, in_=csp_d[2 * j + h2:2 * j + h2 + 1, :].broadcast_to([64, TC]))))(),
                          waits=[("cspst",),
                                 ("pexp", j - 3) if j >= 3 else ("aexp", NE - 3 + j)],
                          inc=(f"bc{j % 3}s", 16))
                    ctx.ms[("bcp", j)] = here(f"bc{j % 3}s")
                    E("scalar", (lambda j=j: (lambda e: e.activation(
                        out=a_sc[j % 2][:], in_=bc[j % 3][:], func=AF.Exp,
                        scale=acont[:, j:j + 1])))(),
                      waits=[("bcp", j),
                             ("corr", j - 2) if j >= 2 else ("scan", NE - 2 + j)],
                      inc=("s", 1), ms=("pexp", j))
                    E("vector", (lambda j=j: (lambda e: e.scalar_tensor_tensor(
                        out=scr[0][:], in0=a_sc[j % 2][:], scalar=hin[:, j:j + 1],
                        in1=u_sb[:, j * TC:(j + 1) * TC], op0=ALU.mult, op1=ALU.add)))(),
                      waits=[("pexp", j), ("hin",), ("ysconv", j - 1) if j >= 1 else None],
                      inc=("v", 1), ms=("corr", j))
                    E("scalar", (lambda j=j: (lambda e: e.copy(
                        out=ysbf[j % 2][:], in_=scr[0][:])))(),
                      waits=[("corr", j), ("ystst", j - 2) if j >= 2 else None],
                      inc=("s", 1), ms=("ysconv", j))
                    E("sync", (lambda j=j: (lambda e: e.dma_start(
                        out=yst_d[j * 128:(j + 1) * 128, :], in_=ysbf[j % 2][:])))(),
                      waits=[("ysconv", j)], inc=(f"yb{j % 2}", 16), ms=("ystst", j))

                p3j = [0]
                p5j = [0]
                for k in range(NI + 2):
                    c = k - 2
                    if c == 6:
                        p4_exchange()
                    i = k + 2
                    if i < NI:
                        p6_load(i)
                    if 0 <= c < NI:
                        p6_compute(c)
                        if c <= 3:
                            for _ in range(4):
                                if p3j[0] < NE:
                                    p3_step(p3j[0])
                                    p3j[0] += 1
                        if c in (9, 11, 13, 15):
                            for _ in range(4):
                                if p5j[0] < NE:
                                    p5_step(p5j[0])
                                    p5j[0] += 1
                pgpu.close()

        if _PH <= 5:
            reg_tap("dbg_u", u_sb[:], [128, NE * TC], F32, ("scan", NE - 1))
            reg_tap("dbg_yst", yst_d, [D, TC], BF16, ("ystst", NE - 1))
            reg_tap("dbg_hl", hl_sb[:], [128, NE], F32, ("scan", NE - 1))
            reg_tap("dbg_achk", achk[:], [128, NE], F32, ("achk",))
            reg_tap("dbg_hin", hin[:], [128, NE], F32, ("hin",))
            reg_tap("dbg_carr", carr[:], [128, 256], F32, ("carr",))
            reg_tap("dbg_sp", sp_d, [NH, TC], F32, ("spst",))
            reg_tap("dbg_csp", csp_d, [NH, TC], F32, ("cspst",))
            reg_tap("dbg_xnt", xnT[:], [128, NE * TC], BF16, ("xnT",))
            return finish()

        b_outer = ExitStack()
        b_outer_holder[0] = b_outer
        with ExitStack() as b1:

            gu_stack.close()
            gu_holder[0] = None
            xnT_stack.close()
            xnT_holder[0] = None
            g_tm = b_outer.enter_context(nc.sbuf_tensor("g_tm", [128, NT * D], BF16, side="right"))
            ysT2 = b_outer.enter_context(nc.sbuf_tensor("ysT2", [128, NE * TC], BF16, side="right"))
            woqbR = b_outer.enter_context(nc.sbuf_tensor("woqbR", [128, NE * 512], BF16, side="right"))
            woqbL = [None, None]

            def wo_buf(ch):
                return woqbR if ch == 0 else woqbL[ch % 2]

            wo_pos = [0]

            def wo_trickle(n):
                for _ in range(n):
                    dd = wo_pos[0]
                    if dd >= NE:
                        return
                    wo_pos[0] += 1
                    E("gpsimd", (lambda dd=dd: (lambda e: e.dma_start(
                        out=woqbR[:, dd * 512:(dd + 1) * 512],
                        in_=ap["wo"][0, dd, :, :])))(),
                      waits=[("gumm", NI - 1) if dd == 0 else None],
                      inc=("wq0", 16))
                    if dd == NE - 1:
                        ctx.ms[("woqd", 0)] = here("wq0")

            def wo_load(ch):
                for dd in range(NE):
                    E("gpsimd", (lambda ch=ch, dd=dd: (lambda e: e.dma_start(
                        out=woqbL[ch % 2][:, dd * 512:(dd + 1) * 512],
                        in_=ap["wo"][ch, dd, :, :])))(),
                      waits=[("ymm", ch - 2) if (ch >= 2 and dd == 0) else None,
                             ("downmm", NE - 1) if (ch == 1 and dd == 0) else None],
                      inc=(f"wq{ch % 2}", 16))
                ctx.ms[("woqd", ch)] = here(f"wq{ch % 2}")
            for dd in range(NE):
                E("sync", (lambda dd=dd: (lambda e: e.dma_start(
                    out=ysT2[:, dd * TC:(dd + 1) * TC],
                    in_=yst_d[dd * 128:(dd + 1) * 128, :])))(),
                  waits=[("gumm", NI - 1) if dd == 0 else None,
                         ("ystst", NE - 2) if dd == 0 else None,
                         ("ystst", NE - 1) if dd == 0 else None],
                  inc=("ysl", 16))
            ctx.ms[("yst2all",)] = here("ysl")

            with ExitStack() as b2:
                dwb = [b2.enter_context(nc.sbuf_tensor(f"dwb{b}", [128, IT], BF16)) for b in range(2)]
                gt_sc = [b2.enter_context(nc.sbuf_tensor(f"gt{b}", [128, TC], BF16)) for b in range(2)]
                with nc.psum_tensor("pd0", [128, TC], F32) as pd0, \
                     nc.psum_tensor("pd1", [128, TC], F32) as pd1:
                    pd = [pd0, pd1]
                    for Et in range(NE):
                        for q in range(2):
                            E("gpsimd", (lambda Et=Et, q=q: (lambda e: e.dma_start(
                                out=dwb[Et % 2][:, q * (IT // 2):(q + 1) * (IT // 2)],
                                in_=ap["dw"][Et, :, q * (IT // 2):(q + 1) * (IT // 2)])))(),
                              waits=[("downmm", Et - 2) if Et >= 2 else None,
                                     ("ystst", NE - 2) if Et == 0 and q == 0 else None,
                                     ("ystst", NE - 1) if Et == 0 and q == 0 else None],
                              inc=(f"dw{Et % 2}", 16))
                        ctx.ms[("dwdma2", Et)] = here(f"dw{Et % 2}")
                        if Et >= 2:
                            wo_trickle(4)
                        for nh in range(2):
                            for i in range(NI):
                                first = (nh == 0 and i == 0)
                                last = (nh == 1 and i == NI - 1)
                                E("tensor", (lambda Et=Et, nh=nh, i=i: (lambda e: e.matmul(
                                    out=pd[Et % 2][:, nh * 512:(nh + 1) * 512],
                                    lhsT=dwb[Et % 2][:, i * 128:(i + 1) * 128],
                                    rhs=hdn_slice(i)[:, nh * 512: nh * 512 + 512],
                                    start=(i == 0), stop=(i == NI - 1))))(),
                                  waits=([("dwdma2", Et), ("hdn", NI - 1),
                                          ("gt", Et - 2) if Et >= 2 else None]
                                         if first else []),
                                  inc=(("t", 1) if last else None),
                                  ms=(("downmm", Et) if last else None))
                        E("scalar", (lambda Et=Et: (lambda e: e.activation(
                            out=gt_sc[Et % 2][:], in_=pd[Et % 2][:], func=AF.Silu,
                            scale=ds_sb[:, Et:Et + 1])))(),
                          waits=[("downmm", Et), ("ldall",),
                                 ("xbar", Et - 2) if Et >= 2 else None],
                          inc=("s", 1), ms=("gt", Et))
                        for tt in range(NT):
                            E("sync", (lambda Et=Et, tt=tt: (lambda e: e.dma_start_transpose(
                                out=g_tm[:, tt * D + Et * 128: tt * D + (Et + 1) * 128],
                                in_=gt_sc[Et % 2][:, tt * 128:(tt + 1) * 128])))(),
                              waits=[("gt", Et)], inc=(f"xb{Et % 2}", 16))
                        ctx.ms[("xbar", Et)] = here(f"xb{Et % 2}")

        u_stack.close()

        if _PH <= 7:
            reg_tap("dbg_hdna", hdnT_a[:], [128, NSPLIT * TC], BF16, ("hdn", NI - 1))
            reg_tap("dbg_hdnb", u_bf[:, 0:(NI - NSPLIT) * TC], [128, (NI - NSPLIT) * TC], BF16, ("hdn", NI - 1))
            reg_tap("dbg_dwb", dwb[(NE - 1) % 2][:], [128, IT], BF16, ("dwdma2", NE - 1))
            reg_tap("dbg_gtm", g_tm[:], [128, NT * D], BF16, ("xbar", NE - 1))
            reg_tap("dbg_yst", yst_d, [D, TC], BF16, ("ystst", NE - 1))
            reg_tap("dbg_hin", hin[:], [128, NE], F32, ("hin",))
            return finish()

        with ExitStack() as c1:
            woqbL[0] = c1.enter_context(nc.sbuf_tensor("woqbL0", [128, NE * 512], BF16))
            woqbL[1] = c1.enter_context(nc.sbuf_tensor("woqbL1", [128, NE * 512], BF16))
            x2c = [c1.enter_context(nc.sbuf_tensor(f"x2c{b}", [128, 512], F32)) for b in range(2)]
            sg_sc = [c1.enter_context(nc.sbuf_tensor(f"sgs{b}", [128, 512], F32)) for b in range(2)]
            outc = [c1.enter_context(nc.sbuf_tensor(f"outc{b}", [128, 512], F32)) for b in range(4)]
            with ExitStack() as cps:
                py = [cps.enter_context(nc.psum_tensor(f"py{tt}", [128, 512], F32))
                      for tt in range(NT)]
                wo_load(1)
                for ch in range(4):
                    if ch + 1 >= 2 and ch + 1 < 4:
                        wo_load(ch + 1)
                    for tt in range(NT):
                        lin = ch * NT + tt
                        for dd in range(NE):
                            first = (dd == 0)
                            last = (dd == NE - 1)
                            E("tensor", (lambda ch=ch, dd=dd, tt=tt: (lambda e: e.matmul(
                                out=py[tt][:],
                                lhsT=ysT2[:, dd * TC + tt * 128: dd * TC + (tt + 1) * 128],
                                rhs=wo_buf(ch)[:, dd * 512:(dd + 1) * 512],
                                start=(dd == 0), stop=(dd == NE - 1))))(),
                              waits=([("woqd", ch),
                                      ("yst2all",),
                                      ("gt", NE - 1) if lin == 0 else None,
                                      ("sgmul", ch - 1, tt) if ch >= 1 else None]
                                     if first else []),
                              inc=(("t", 1) if last else None),
                              ms=(("ymmT", ch, tt) if last else None))
                        E("gpsimd", (lambda ch=ch, tt=tt, lin=lin: (lambda e: e.dma_start(
                            out=x2c[lin % 2][:],
                            in_=ap["x"][tt * 128:(tt + 1) * 128, ch * 512:(ch + 1) * 512])))(),
                          waits=[("sgst", (lin - 2) // NT, (lin - 2) % NT) if lin >= 2 else
                                 ("downmm", NE - 1),
                                 ("gt", NE - 1) if lin == 0 else None],
                          inc=(f"x2{lin % 2}", 16), ms=("x2", ch, tt))
                        E("vector", (lambda ch=ch, tt=tt: (lambda e: e.tensor_tensor(
                            out=sg_sc[tt % 2][:], in0=py[tt][:],
                            in1=g_tm[:, tt * D + ch * 512: tt * D + ch * 512 + 512],
                            op=ALU.mult)))(),
                          waits=[("ymmT", ch, tt),
                                 ("xbar", NE - 2), ("xbar", NE - 1),
                                 ("sgst", (lin - 2) // NT, (lin - 2) % NT) if lin >= 2 else None],
                          inc=("v", 1), ms=("sgmul", ch, tt))
                        E("vector", (lambda ch=ch, tt=tt, lin=lin: (lambda e: e.tensor_tensor(
                            out=outc[tt % 4][:], in0=sg_sc[tt % 2][:], in1=x2c[lin % 2][:],
                            op=ALU.add)))(),
                          waits=[("x2", ch, tt),
                                 ("outst", (lin - 4) // NT, (lin - 4) % NT) if lin >= 4 else None],
                          inc=("v", 1), ms=("sgst", ch, tt))
                        E("sync", (lambda ch=ch, tt=tt: (lambda e: e.dma_start(
                            out=ap["out"][tt * 128:(tt + 1) * 128, ch * 512:(ch + 1) * 512],
                            in_=outc[tt % 4][:])))(),
                          waits=[("sgst", ch, tt)], inc=(f"os{tt % 4}", 16), ms=("outst", ch, tt))
                    ctx.ms[("ymm", ch)] = ctx.ms[("ymmT", ch, NT - 1)]

        if DEBUG:
            reg_tap("dbg_yst", yst_d, [D, TC], BF16, ("ystst", NE - 1))
            reg_tap("dbg_gtm", g_tm[:], [128, NT * D], BF16, ("xbar", NE - 1))
            reg_tap("dbg_hin", hin[:], [128, NE], F32, ("hin",))
            reg_tap("dbg_achk", achk[:], [128, NE], F32, ("achk",))
            reg_tap("dbg_hl", hl_sb[:], [128, NE], F32, ("scan", NE - 1))
            reg_tap("dbg_carr", carr[:], [128, 256], F32, ("carr",))
            reg_tap("dbg_sp", sp_d, [NH, TC], F32, ("spst",))
            reg_tap("dbg_csp", csp_d, [NH, TC], F32, ("cspst",))
        return finish()


def _prep_inputs(kw):
    f = np.ascontiguousarray
    W_in = np.asarray(kw["L0_attn_W_in"], np.float32)
    W_out = np.asarray(kw["L0_attn_W_out"], np.float32)
    log_A = np.asarray(kw["L0_attn_log_A"], np.float32)
    W_dt = np.asarray(kw["L0_attn_W_dt"], np.float32)
    b_dt = np.asarray(kw["L0_attn_b_dt"], np.float32)
    gate_w = np.asarray(kw["L0_ffn_gate_w"], np.float32)
    up_w = np.asarray(kw["L0_ffn_up_w"], np.float32)
    down_w = np.asarray(kw["L0_ffn_down_w"], np.float32)
    gate_s = np.asarray(kw["L0_ffn_gate_s"], np.float32)
    up_s = np.asarray(kw["L0_ffn_up_s"], np.float32)
    down_s = np.asarray(kw["L0_ffn_down_s"], np.float32)

    com = {
        "w1": f(W_in.reshape(NE, 128, NE, 128).transpose(0, 3, 2, 1).reshape(NE, 128, D)),
        "wdt": f(W_dt.T.reshape(NE, 128, NH)),
        "bdt": f(b_dt.reshape(NH, 1)),
        "alog": f(log_A.reshape(NE, 128).T),
        "gw": f(gate_w.reshape(NI, 128, NE, 128).transpose(0, 3, 2, 1).reshape(NI, 128, D)),
        "uw": f(up_w.reshape(NI, 128, NE, 128).transpose(0, 3, 2, 1).reshape(NI, 128, D)),
        "dw": f(down_w.reshape(NE, 128, NI, 128).transpose(0, 3, 2, 1).reshape(NE, 128, IT)),
        "wo": f(W_out.reshape(4, 512, NE, 128).transpose(0, 2, 3, 1)),
        "gsc": f(gate_s.reshape(NI, 128).T),
        "usc": f(up_s.reshape(NI, 128).T),
        "dsc": f(down_s.reshape(NE, 128).T),
        "ident": np.eye(128, dtype=np.float32),
    }
    x_full = np.asarray(kw["L0_attn_x"], np.float32).reshape(NCORE * TC, D)
    in_maps = []
    for r in range(NCORE):
        m = np.zeros((128, 128), np.float32)
        for j in range(NCORE):
            if j < r:
                m[:, j * 16:(j + 1) * 16] = 1.0
        im = 1.0 - m
        d = dict(com)
        d["x"] = f(x_full[r * TC:(r + 1) * TC, :])
        d["msk"] = m
        d["imsk"] = im
        in_maps.append(d)
    return in_maps


_CACHE = {}
DEBUG = False
_PH = 8


def kernel(**inputs):
    in_maps = _prep_inputs(inputs)
    key = ("nc", DEBUG, _PH)
    if key not in _CACHE:
        _, ctx1 = build(dry=True)
        _CACHE[key] = build(dry=False, M=ctx1.ms)[0]
    nc = _CACHE[key]
    res = run_bass_kernel_spmd(nc, in_maps, core_ids=list(range(NCORE)))
    _CACHE["last_results"] = res.results
    out = np.concatenate([r["out"] for r in res.results], axis=0)
    return out.reshape(1, NCORE * TC, D).astype(np.float32)


# version 18
# speedup vs baseline: 1.0167x; 1.0129x over previous
import numpy as np
import ml_dtypes
from contextlib import ExitStack

import concourse.bass as bass
import concourse.mybir as mybir
from concourse.bass_utils import run_bass_kernel_spmd

F32 = mybir.dt.float32
BF16 = mybir.dt.bfloat16
AF = mybir.ActivationFunctionType
ALU = mybir.AluOpType
AX = mybir.AxisListType

NCORE = 8
TC = 1024
D = 2048
NE = 16
NT = 8
NI = 44
NH = 32
IT = 5632
DWQ = 1408
NSPLIT = 12

ENGS = ("sync", "gpsimd", "scalar", "vector", "tensor")
OWN_SEM = {"vector": "v", "scalar": "s"}


class Ctx:
    def __init__(self, dry, M=None):
        self.dry = dry
        self.M = M or {}
        self.ctr = {}
        self.ms = {}
        self.eng_ops = {e: [] for e in ENGS}

    def emit(self, eng, fn=None, waits=(), inc=None, ms=None):
        if inc is None and eng in OWN_SEM:
            inc = (OWN_SEM[eng], 1)
        if inc is not None:
            sem, amt = inc
            self.ctr[sem] = self.ctr.get(sem, 0) + amt
            if ms is not None:
                self.ms[ms] = (sem, self.ctr[sem])
        if not self.dry:
            rw = []
            for k in waits:
                if k is None:
                    continue
                if k not in self.M:
                    raise KeyError(f"milestone {k} not in pass-1 table")
                rw.append(self.M[k])
            self.eng_ops[eng].append((fn, tuple(rw), inc))

    def w(self, key):
        return key


def _body(engname, ops, sems):
    own = OWN_SEM.get(engname)

    def body(e):
        waited = {}
        own_count = 0
        for fn, waits, inc in ops:
            if own is not None and own_count > 0:
                if own_count > waited.get(own, 0):
                    e.wait_ge(sems[own], own_count)
                    waited[own] = own_count
            for semname, val in waits:
                if val <= waited.get(semname, 0):
                    continue
                e.wait_ge(sems[semname], val)
                waited[semname] = val
            instr = fn(e)
            if inc is not None:
                instr.then_inc(sems[inc[0]], inc[1])
                if own is not None and inc[0] == own:
                    own_count += inc[1]
    return body


def build(dry, M=None):
    nc = bass.Bass("TRN2", target_bir_lowering=False)
    ctx = Ctx(dry, M)

    ap = {}
    ap["x"] = nc.dram_tensor("x", [TC, D], F32, kind="ExternalInput").ap()
    ap["w1"] = nc.dram_tensor("w1", [NE, 128, D], F32, kind="ExternalInput").ap()
    ap["wdt"] = nc.dram_tensor("wdt", [NE, 128, NH], F32, kind="ExternalInput").ap()
    ap["bdt"] = nc.dram_tensor("bdt", [NH, 1], F32, kind="ExternalInput").ap()
    ap["alog"] = nc.dram_tensor("alog", [128, NE], F32, kind="ExternalInput").ap()
    ap["gw"] = nc.dram_tensor("gw", [NI, 128, D], F32, kind="ExternalInput").ap()
    ap["uw"] = nc.dram_tensor("uw", [NI, 128, D], F32, kind="ExternalInput").ap()
    ap["dw"] = nc.dram_tensor("dw", [NE, 128, IT], F32, kind="ExternalInput").ap()
    ap["wo"] = nc.dram_tensor("wo", [4, NE, 128, 512], F32, kind="ExternalInput").ap()
    ap["gsc"] = nc.dram_tensor("gsc", [128, NI], F32, kind="ExternalInput").ap()
    ap["usc"] = nc.dram_tensor("usc", [128, NI], F32, kind="ExternalInput").ap()
    ap["dsc"] = nc.dram_tensor("dsc", [128, NE], F32, kind="ExternalInput").ap()
    ap["msk"] = nc.dram_tensor("msk", [128, 128], F32, kind="ExternalInput").ap()
    ap["imsk"] = nc.dram_tensor("imsk", [128, 128], F32, kind="ExternalInput").ap()
    ap["ident"] = nc.dram_tensor("ident", [128, 128], F32, kind="ExternalInput").ap()
    ap["out"] = nc.dram_tensor("out", [TC, D], F32, kind="ExternalOutput").ap()
    sp_d = nc.dram_tensor("sp_d", [NH, TC], F32).ap()
    csp_d = nc.dram_tensor("csp_d", [NH, TC], F32).ap()
    yst_d = nc.dram_tensor("yst_d", [D, TC], BF16).ap()
    cc_in = nc.dram_tensor("cc_in", [128, 32], F32)
    cc_res = nc.dram_tensor("cc_res", [NCORE * 128, 32], F32)

    E = ctx.emit

    with ExitStack() as outer:
        def sb(name, shape, dt, stack=outer):
            return stack.enter_context(nc.sbuf_tensor(name, shape, dt))

        b_outer_holder = [None]
        xnT_holder = [None]
        gu_holder = [None]
        xnT_stack = ExitStack()
        xnT_holder[0] = xnT_stack
        xnT = xnT_stack.enter_context(nc.sbuf_tensor("xnT", [128, NE * TC], BF16, side="right"))
        hdnT_a = outer.enter_context(nc.sbuf_tensor("hdnT_a", [128, NSPLIT * TC], BF16))

        identf = sb("identf", [128, 128], F32)
        identb = sb("identb", [128, 128], BF16)
        msk_sb = sb("msk_sb", [128, 128], F32)
        imsk_sb = sb("imsk_sb", [128, 128], F32)
        alog_sb = sb("alog_sb", [128, NE], F32)
        aexp_sb = sb("aexp_sb", [128, NE], F32)
        acont = sb("acont", [128, NE], F32)
        gs_sb = sb("gs_sb", [128, NI], F32)
        us_sb = sb("us_sb", [128, NI], F32)
        ds_sb = sb("ds_sb", [128, NE], F32)
        bdt_sb = sb("bdt_sb", [NH, 1], F32)
        wdt_sb = sb("wdt_sb", [128, NE * NH], F32)
        hl_sb = sb("hl_sb", [128, NE], F32)
        alast = sb("alast", [128, NE], F32)
        achkp = sb("achkp", [128, NE], F32)
        achk = sb("achk", [128, NE], F32)
        hin = sb("hin", [128, NE], F32)
        wk16a = sb("wk16a", [128, NE], F32)
        wk16b = sb("wk16b", [128, NE], F32)
        carr = sb("carr", [128, 256], F32)
        red = sb("red", [128, NT], F32)
        negm = sb("negm", [128, NT], F32)
        ssq = sb("ssq", [128, NT], F32)
        veps = sb("veps", [128, NT], F32)
        lnv = sb("lnv", [128, NT], F32)
        rstd = sb("rstd", [128, NT], F32)
        gu_stack = ExitStack()
        gu_holder[0] = gu_stack
        gwb = [gu_stack.enter_context(nc.sbuf_tensor(f"gwb{b}", [128, D], BF16, side="right")) for b in range(3)]
        uwb = [gu_stack.enter_context(nc.sbuf_tensor(f"uwb{b}", [128, D], BF16, side="right")) for b in range(3)]
        w1b = [gu_stack.enter_context(nc.sbuf_tensor(f"w1b{b}", [128, D], BF16, side="right")) for b in range(2)]
        sgb = [gu_stack.enter_context(nc.sbuf_tensor(f"sgb{b}", [128, TC], BF16, side="right")) for b in range(2)]
        uub = [gu_stack.enter_context(nc.sbuf_tensor(f"uub{b}", [128, TC], BF16, side="right")) for b in range(2)]

        SEMNAMES = ("sml", "xt0s", "xt1s", "xt2s", "xt3s", "xt4s", "xt5s", "xt6s", "xt7s", "w1s", "sps", "als", "bc0s", "bc1s",
                    "sm2", "bc2s", "cci", "cc", "car", "yb0", "yb1", "gw0", "gw1", "gw2", "uw0", "uw1", "uw2",
                    "dw0", "dw1", "xb0", "xb1", "ysl", "wq0", "wq1", "x20", "x21",
                    "os0", "os1", "os2", "os3", "dbs", "t", "v", "s")
        sems = {n: outer.enter_context(nc.semaphore(n)) for n in SEMNAMES}

        def here(sem):
            return (sem, ctx.ctr.get(sem, 0))

        def p6_load(i):
            E("gpsimd", (lambda i=i: (lambda e: e.dma_start(
                out=gwb[i % 3][:], in_=ap["gw"][i, :, :])))(),
              waits=[("gumm", i - 3) if i >= 3 else None],
              inc=(f"gw{i % 3}", 16), ms=("gdmah", i, 0))
            E("gpsimd", (lambda i=i: (lambda e: e.dma_start(
                out=uwb[i % 3][:], in_=ap["uw"][i, :, :])))(),
              waits=[("gumm", i - 3) if i >= 3 else None],
              inc=(f"uw{i % 3}", 16), ms=("udmah", i, 0))

        taps = []

        def reg_tap(name, src, shape, dt_, wait):
            taps.append((name, src, shape, dt_, wait))

        def finish():
            for name, src, shape, dt_, wait in taps:
                dout = nc.dram_tensor(name, shape, dt_, kind="ExternalOutput").ap()
                E("gpsimd", (lambda dout=dout, src=src: (lambda e: e.dma_start(
                    out=dout, in_=src)))(),
                  waits=[wait], inc=("dbs", 16))
            finals = []
            for sname in SEMNAMES:
                if sname in ("t", "v", "s", "cc"):
                    continue
                tot = ctx.ctr.get(sname, 0)
                if tot:
                    ctx.ms[("fin", sname)] = (sname, tot)
                    finals.append(("fin", sname))
            if finals:
                E("sync", lambda e: e.nop(), waits=finals)
                E("gpsimd", lambda e: e.nop(), waits=finals)
            if not dry:
                with nc.Block() as block:
                    block.sync(_body("sync", ctx.eng_ops["sync"], sems))
                    block.gpsimd(_body("gpsimd", ctx.eng_ops["gpsimd"], sems))
                    block.scalar(_body("scalar", ctx.eng_ops["scalar"], sems))
                    block.vector(_body("vector", ctx.eng_ops["vector"], sems))
                    block.tensor(_body("tensor", ctx.eng_ops["tensor"], sems))
            for st_ in (b_outer_holder, xnT_holder, gu_holder):
                if st_ and st_[0] is not None:
                    st_[0].close()
                    st_[0] = None
            return nc, ctx

        E("sync", lambda e: e.dma_start(out=identf[:], in_=ap["ident"]),
          inc=("sml", 16))
        E("sync", lambda e: e.dma_start(out=alog_sb[:], in_=ap["alog"]),
          inc=("sml", 16), ms=("identld",))

        def small_loads():
            small = [("msk", msk_sb[:], ap["msk"]),
                     ("imsk", imsk_sb[:], ap["imsk"]),
                     ("gs", gs_sb[:], ap["gsc"]), ("us", us_sb[:], ap["usc"]),
                     ("ds", ds_sb[:], ap["dsc"]), ("bdt", bdt_sb[:], ap["bdt"])]
            for name, dst, srcap in small:
                E("gpsimd", (lambda d=dst, s=srcap: (lambda e: e.dma_start(out=d, in_=s)))(),
                  inc=("sm2", 16))
            for Ew in range(NE):
                E("gpsimd", (lambda Ew=Ew: (lambda e: e.dma_start(
                    out=wdt_sb[:, Ew * NH:(Ew + 1) * NH], in_=ap["wdt"][Ew, :, :])))(),
                  inc=("sm2", 16))
            ctx.ms[("ldall",)] = here("sm2")

        E("scalar", lambda e: e.copy(out=identb[:], in_=identf[:]),
          waits=[("identld",)], inc=("s", 1), ms=("identb",))
        E("scalar", lambda e: e.activation(out=aexp_sb[:], in_=alog_sb[:], func=AF.Exp),
          waits=[("identld",)], inc=("s", 1), ms=("aexpc",))
        E("vector", lambda e: e.tensor_scalar(out=acont[:], in0=aexp_sb[:], scalar1=-1.0,
                                              scalar2=None, op0=ALU.mult),
          waits=[("aexpc",)], inc=("v", 1), ms=("acont",))

        with ExitStack() as a1:
            xt = [a1.enter_context(nc.sbuf_tensor(f"xt{b}", [128, D], F32)) for b in range(NT)]
            sq = a1.enter_context(nc.sbuf_tensor("sq", [128, D], F32))
            av = a1.enter_context(nc.sbuf_tensor("av", [128, NT], F32))
            m2 = a1.enter_context(nc.sbuf_tensor("m2", [128, NT], F32))
            xnb = [a1.enter_context(nc.sbuf_tensor(f"xnb{b}", [128, D], BF16)) for b in range(2)]
            with nc.psum_tensor("ps_tr0", [128, 128], BF16) as pt0, \
                 nc.psum_tensor("ps_tr1", [128, 128], BF16) as pt1:
                ps_tr = [pt0, pt1]
                def p1_stats(tt):
                    E("sync", (lambda tt=tt: (lambda e: e.dma_start(
                        out=xt[tt][:], in_=ap["x"][tt * 128:(tt + 1) * 128, :])))(),
                      inc=(f"xt{tt}s", 16), ms=("xload", tt))
                    E("vector", (lambda tt=tt: (lambda e: e.reduce_sum(
                        out=red[:, tt:tt + 1], in_=xt[tt][:], axis=AX.X)))(),
                      waits=[("xload", tt)], inc=("v", 1), ms=("red", tt))
                    E("scalar", (lambda tt=tt: (lambda e: e.activation(
                        out=sq[:], in_=xt[tt][:], func=AF.Square,
                        accum_out=ssq[:, tt:tt + 1])))(),
                      waits=[("xload", tt)], inc=("s", 1), ms=("ssq", tt))

                def p1_norm(tt):
                    E("vector", (lambda tt=tt: (lambda e: e.tensor_scalar(
                        out=negm[:, tt:tt + 1], in0=red[:, tt:tt + 1],
                        scalar1=-1.0 / D, scalar2=None, op0=ALU.mult)))(),
                      waits=[("red", tt)], inc=("v", 1), ms=("negm", tt))
                    E("vector", (lambda tt=tt: (lambda e: e.tensor_scalar(
                        out=av[:, tt:tt + 1], in0=ssq[:, tt:tt + 1],
                        scalar1=1.0 / D, scalar2=1e-5, op0=ALU.mult, op1=ALU.add)))(),
                      waits=[("ssq", tt)], inc=("v", 1))
                    E("vector", (lambda tt=tt: (lambda e: e.tensor_tensor(
                        out=m2[:, tt:tt + 1], in0=negm[:, tt:tt + 1],
                        in1=negm[:, tt:tt + 1], op=ALU.mult)))(),
                      inc=("v", 1))
                    E("vector", (lambda tt=tt: (lambda e: e.tensor_sub(
                        out=veps[:, tt:tt + 1], in0=av[:, tt:tt + 1],
                        in1=m2[:, tt:tt + 1])))(),
                      inc=("v", 1), ms=("veps", tt))
                    E("scalar", (lambda tt=tt: (lambda e: e.activation(
                        out=lnv[:, tt:tt + 1], in_=veps[:, tt:tt + 1], func=AF.Ln)))(),
                      waits=[("veps", tt)], inc=("s", 1))
                    E("scalar", (lambda tt=tt: (lambda e: e.activation(
                        out=rstd[:, tt:tt + 1], in_=lnv[:, tt:tt + 1], func=AF.Exp,
                        scale=-0.5)))(),
                      inc=("s", 1), ms=("rstd", tt))
                    E("vector", (lambda tt=tt: (lambda e: e.tensor_scalar(
                        out=xnb[tt % 2][:], in0=xt[tt][:], scalar1=negm[:, tt:tt + 1],
                        scalar2=rstd[:, tt:tt + 1], op0=ALU.add, op1=ALU.mult)))(),
                      waits=[("rstd", tt), ("trdone", tt - 2) if tt >= 2 else None],
                      inc=("v", 1), ms=("xnb", tt))

                def p1_transposes(tt):
                    for d in range(NE):
                        k = (tt * NE + d) % 2
                        E("tensor", (lambda tt=tt, d=d, k=k: (lambda e: e.transpose(
                            out=ps_tr[k][:], in_=xnb[tt % 2][:, d * 128:(d + 1) * 128],
                            identity=identb[:])))(),
                          waits=[("xnb", tt), ("identb",), ("trcopy", tt, d - 2) if d >= 2
                                 else (("trcopy", tt - 1, NE - 2 + d) if tt >= 1 else None)],
                          inc=("t", 1), ms=("tr", tt, d))
                        if d % 2 == 0:
                            E("vector", (lambda tt=tt, d=d, k=k: (lambda e: e.tensor_copy(
                                out=xnT[:, d * TC + tt * 128: d * TC + (tt + 1) * 128],
                                in_=ps_tr[k][:])))(),
                              waits=[("tr", tt, d)], inc=("v", 1), ms=("trcopy", tt, d))
                        else:
                            E("scalar", (lambda tt=tt, d=d, k=k: (lambda e: e.copy(
                                out=xnT[:, d * TC + tt * 128: d * TC + (tt + 1) * 128],
                                in_=ps_tr[k][:])))(),
                              waits=[("tr", tt, d)], inc=("s", 1), ms=("trcopy", tt, d))
                    ctx.ms[("trdone", tt)] = ctx.ms[("trcopy", tt, NE - 1)]

                p1_stats(0)
                p1_stats(1)
                for tt in range(NT):
                    if tt + 2 < NT:
                        p1_stats(tt + 2)
                    p1_norm(tt)
                    p1_transposes(tt)
            ctx.ms[("xnT",)] = ctx.ms[("trcopy", NT - 1, NE - 1)]
            ctx.ms[("xnTv",)] = ctx.ms[("trcopy", NT - 1, NE - 2)]

        if _PH <= 1:
            reg_tap("dbg_red", red[:], [128, NT], F32, ("xnT",))
            reg_tap("dbg_negm", negm[:], [128, NT], F32, ("xnT",))
            reg_tap("dbg_ssq", ssq[:], [128, NT], F32, ("xnT",))
            reg_tap("dbg_veps", veps[:], [128, NT], F32, ("xnT",))
            reg_tap("dbg_rstd", rstd[:], [128, NT], F32, ("xnT",))
            reg_tap("dbg_xnt", xnT[:], [128, NE * TC], BF16, ("xnT",))
            return finish()

        u_stack = ExitStack()
        with ExitStack() as a2:
            u_sb = u_stack.enter_context(nc.sbuf_tensor("u_sb", [128, NE * TC], F32))
            u_bf = u_sb.bitcast(BF16)
            with ExitStack() as a2w:
                zch = [a2w.enter_context(nc.sbuf_tensor(f"zch{i}", [NH, TC], F32)) for i in range(4)]
                z_sb, zs1, zs2, zs3 = zch
                csp_sb = a2w.enter_context(nc.sbuf_tensor("csp_sb", [NH, TC], F32))

                with nc.psum_tensor("pmm0", [128, TC], F32) as pm0, \
                     nc.psum_tensor("pmm1", [128, TC], F32) as pm1, \
                     nc.psum_tensor("pdt", [NH, TC], F32) as pdt:
                    pmm = [pm0, pm1]
                    p6_load(0)
                    p6_load(1)
                    for Et in range(NE):
                        if Et == 2:
                            small_loads()
                        E("gpsimd", (lambda Et=Et: (lambda e: e.dma_start(
                            out=w1b[Et % 2][:], in_=ap["w1"][Et, :, :])))(),
                          waits=[("mm1", Et - 2) if Et >= 2 else None],
                          inc=(f"gw{Et % 2}", 16), ms=("w1dma", Et))
                        for nh in range(2):
                            for dd in range(NE):
                                first = (nh == 0 and dd == 0)
                                last = (nh == 1 and dd == NE - 1)
                                E("tensor", (lambda Et=Et, nh=nh, dd=dd: (lambda e: e.matmul(
                                    out=pmm[Et % 2][:, nh * 512:(nh + 1) * 512],
                                    lhsT=w1b[Et % 2][:, dd * 128:(dd + 1) * 128],
                                    rhs=xnT[:, dd * TC + nh * 512: dd * TC + nh * 512 + 512],
                                    start=(dd == 0), stop=(dd == NE - 1))))(),
                                  waits=([("w1dma", Et), ("xnT",), ("xnTv",),
                                          ("uev", Et - 2) if Et >= 2 else None]
                                         if first else []),
                                  inc=(("t", 1) if last else None),
                                  ms=(("mm1", Et) if last else None))
                        E("vector", (lambda Et=Et: (lambda e: e.tensor_copy(
                            out=u_sb[:, Et * TC:(Et + 1) * TC], in_=pmm[Et % 2][:])))(),
                          waits=[("mm1", Et)], inc=("v", 1), ms=("uev", Et))
                    for nh in range(2):
                        for Et in range(NE):
                            first = (nh == 0 and Et == 0)
                            last = (nh == 1 and Et == NE - 1)
                            E("tensor", (lambda Et=Et, nh=nh: (lambda e: e.matmul(
                                out=pdt[:, nh * 512:(nh + 1) * 512],
                                lhsT=wdt_sb[:, Et * NH:(Et + 1) * NH],
                                rhs=u_sb[:, Et * TC + nh * 512: Et * TC + nh * 512 + 512],
                                start=(Et == 0), stop=(Et == NE - 1))))(),
                              waits=([("uev", NE - 1), ("ldall",)] if first else []),
                              inc=(("t", 1) if last else None),
                              ms=(("dtmm",) if last else None))
                    E("vector", lambda e: e.tensor_scalar(out=z_sb[:], in0=pdt[:],
                                                          scalar1=bdt_sb[:, 0:1],
                                                          scalar2=None, op0=ALU.add),
                      waits=[("dtmm",), ("ldall",)], inc=("v", 1), ms=("z",))
                E("vector", lambda e: e.tensor_scalar(out=zs1[:], in0=z_sb[:], scalar1=-1.0,
                                                      scalar2=None, op0=ALU.mult),
                  waits=[("z",)], inc=("v", 1))
                E("vector", lambda e: e.tensor_tensor(out=zs2[:], in0=z_sb[:], in1=zs1[:],
                                                      op=ALU.min),
                  inc=("v", 1))
                E("vector", lambda e: e.tensor_scalar(out=zs1[:], in0=z_sb[:], scalar1=0.0,
                                                      scalar2=None, op0=ALU.max),
                  inc=("v", 1), ms=("zp",))
                E("scalar", lambda e: e.activation(out=zs3[:], in_=zs2[:], func=AF.Exp),
                  waits=[("zp",)], inc=("s", 1))
                E("scalar", lambda e: e.activation(out=zs2[:], in_=zs3[:], func=AF.Ln, bias=1.0),
                  inc=("s", 1), ms=("l1",))
                E("vector", lambda e: e.tensor_tensor(out=z_sb[:], in0=zs1[:], in1=zs2[:],
                                                      op=ALU.add),
                  waits=[("l1",)], inc=("v", 1), ms=("sp",))
                E("vector", lambda e: e.memset(zs3[:], 0.0), inc=("v", 1))
                E("vector", lambda e: e.tensor_tensor_scan(out=csp_sb[:], data0=z_sb[:],
                                                           data1=zs3[:], initial=0.0,
                                                           op0=ALU.add, op1=ALU.add),
                  inc=("v", 1), ms=("csp",))
                E("sync", lambda e: e.dma_start(out=sp_d, in_=z_sb[:]),
                  waits=[("sp",)], inc=("sps", 16))
                E("sync", lambda e: e.dma_start(out=csp_d, in_=csp_sb[:]),
                  waits=[("csp",)], inc=("sps", 16))
                ctx.ms[("spst",)] = ctx.ms[("cspst",)] = here("sps")
                for j in range(NE):
                    for h2 in range(2):
                        hidx = 2 * j + h2
                        dst = alast[0:64, j:j + 1] if h2 == 0 else alast[64:128, j:j + 1]
                        E("gpsimd", (lambda dst=dst, hidx=hidx: (lambda e: e.dma_start(
                            out=dst, in_=csp_d[hidx:hidx + 1, TC - 1:TC].broadcast_to([64, 1]))))(),
                          waits=[("cspst",)], inc=("als", 16))
                ctx.ms[("alast",)] = here("als")
                E("vector", lambda e: e.tensor_tensor(out=achkp[:], in0=acont[:], in1=alast[:],
                                                      op=ALU.mult),
                  waits=[("alast",), ("acont",)], inc=("v", 1), ms=("achkp",))
                E("scalar", lambda e: e.activation(out=achk[:], in_=achkp[:], func=AF.Exp),
                  waits=[("achkp",)], inc=("s", 1), ms=("achk",))

            pgpu = ExitStack()
            pg = [pgpu.enter_context(nc.psum_tensor(f"pg{b}", [128, TC], F32)) for b in range(2)]
            pu = [pgpu.enter_context(nc.psum_tensor(f"pu{b}", [128, TC], F32)) for b in range(2)]

            def hdn_slice(i):
                if i < NSPLIT:
                    return hdnT_a[:, i * TC:(i + 1) * TC]
                m = i - NSPLIT
                return u_bf[:, m * TC:(m + 1) * TC]

            def p6_compute(i):
                for which, psum, wsb in (("g", pg, gwb), ("u", pu, uwb)):
                    for nh in range(2):
                        for hh in range(NE):
                            first = (which == "g" and nh == 0 and hh == 0)
                            firstu = (which == "u" and nh == 0 and hh == 0)
                            E("tensor", (lambda i=i, which=which, nh=nh, hh=hh, psum=psum, wsb=wsb:
                                         (lambda e: e.matmul(
                                out=psum[i % 2][:, nh * 512:(nh + 1) * 512],
                                lhsT=wsb[i % 3][:, hh * 128:(hh + 1) * 128],
                                rhs=xnT[:, hh * TC + nh * 512: hh * TC + nh * 512 + 512],
                                start=(hh == 0), stop=(hh == NE - 1))))(),
                              waits=([("gdmah", i, 0), ("udmah", i, 0),
                                      ("silu", i - 2) if i >= 2 else None,
                                      ("uu", i - 2) if i >= 2 else None,
                                      ("uev", NE - 1) if i == 0 else None]
                                     if first else
                                     ([("z",)] if (firstu and i == 0) else [])),
                              inc=(("t", 1) if (nh == 1 and hh == NE - 1) else None),
                              ms=((("gmm", i) if which == "g" else ("gumm", i))
                                  if (nh == 1 and hh == NE - 1) else None))
                E("scalar", (lambda i=i: (lambda e: e.activation(
                    out=sgb[i % 2][:], in_=pg[i % 2][:], func=AF.Silu,
                    scale=gs_sb[:, i:i + 1])))(),
                  waits=[("gmm", i), ("ldall",), ("hdn", i - 2) if i >= 2 else None],
                  inc=("s", 1), ms=("silu", i))
                E("vector", (lambda i=i: (lambda e: e.tensor_scalar(
                    out=uub[i % 2][:], in0=pu[i % 2][:], scalar1=us_sb[:, i:i + 1],
                    scalar2=None, op0=ALU.mult)))(),
                  waits=[("gumm", i), ("ldall",), ("hdn", i - 2) if i >= 2 else None],
                  inc=("v", 1), ms=("uu", i))
                E("vector", (lambda i=i: (lambda e: e.tensor_tensor(
                    out=hdn_slice(i), in0=sgb[i % 2][:], in1=uub[i % 2][:],
                    op=ALU.mult)))(),
                  waits=[("silu", i),
                         ("corr", (i - NSPLIT) // 2) if i >= NSPLIT else None],
                  inc=("v", 1), ms=("hdn", i))

            if _PH <= 2:
                reg_tap("dbg_u", u_sb[:], [128, NE * TC], F32, ("uev", NE - 1))
                reg_tap("dbg_z", z_sb[:], [NH, TC], F32, ("z",))
                reg_tap("dbg_sp", sp_d, [NH, TC], F32, ("spst",))
                reg_tap("dbg_csp", csp_d, [NH, TC], F32, ("cspst",))
                reg_tap("dbg_xnt", xnT[:], [128, NE * TC], BF16, ("xnT",))
                return finish()

            with ExitStack() as a3:
                bc = [a3.enter_context(nc.sbuf_tensor(f"bc{b}", [128, TC], F32)) for b in range(3)]
                a_sc = [a3.enter_context(nc.sbuf_tensor(f"a_sc{b}", [128, TC], F32)) for b in range(2)]
                scr = [a3.enter_context(nc.sbuf_tensor("scr0", [128, TC], F32))]
                ysbf = [a3.enter_context(nc.sbuf_tensor(f"ysbf{b}", [128, TC], BF16)) for b in range(2)]

                def p3_step(j):
                    for h2 in range(2):
                        dst = bc[j % 3][0:64, :] if h2 == 0 else bc[j % 3][64:128, :]
                        E("sync", (lambda dst=dst, j=j, h2=h2: (lambda e: e.dma_start(
                            out=dst, in_=sp_d[2 * j + h2:2 * j + h2 + 1, :].broadcast_to([64, TC]))))(),
                          waits=[("spst",), ("aexp", j - 3) if j >= 3 else None],
                          inc=(f"bc{j % 3}s", 16))
                    ctx.ms[("bca", j)] = here(f"bc{j % 3}s")
                    E("scalar", (lambda j=j: (lambda e: e.activation(
                        out=a_sc[j % 2][:], in_=bc[j % 3][:], func=AF.Exp,
                        scale=acont[:, j:j + 1])))(),
                      waits=[("bca", j), ("acont",), ("scan", j - 2) if j >= 2 else None],
                      inc=("s", 1), ms=("aexp", j))
                    E("vector", (lambda j=j: (lambda e: e.tensor_tensor_scan(
                        out=scr[0][:], data0=a_sc[j % 2][:],
                        data1=u_sb[:, j * TC:(j + 1) * TC], initial=0.0,
                        op0=ALU.mult, op1=ALU.add)))(),
                      waits=[("aexp", j), ("dtmm",)],
                      inc=("v", 1))
                    E("vector", (lambda j=j: (lambda e: e.tensor_copy(
                        out=hl_sb[:, j:j + 1], in_=scr[0][:, TC - 1:TC])))(),
                      inc=("v", 1))
                    E("vector", (lambda j=j: (lambda e: e.tensor_copy(
                        out=u_sb[:, j * TC:(j + 1) * TC], in_=scr[0][:])))(),
                      inc=("v", 1), ms=("scan", j))

                def p4_exchange():
                    E("sync", lambda e: e.dma_start(out=cc_in[:, 0:16], in_=achk[:]),
                      waits=[("achk",)], inc=("cci", 16))
                    E("sync", lambda e: e.dma_start(out=cc_in[:, 16:32], in_=hl_sb[:]),
                      waits=[("scan", NE - 1)], inc=("cci", 16), ms=("ccin",))
                    E("gpsimd", lambda e: e.collective_compute(
                        "AllGather", ALU.bypass,
                        replica_groups=[list(range(NCORE))],
                        ins=[cc_in.ap().opt()], outs=[cc_res.ap().opt()]),
                      waits=[("ccin",)], inc=("cc", 1), ms=("ccgo",))
                    for r in range(NCORE):
                        E("gpsimd", (lambda r=r: (lambda e: e.dma_start(
                            out=carr[:, r * 32:(r + 1) * 32],
                            in_=cc_res[r * 128:(r + 1) * 128, :])))(),
                          waits=[("ccgo",)], inc=("car", 16))
                    ctx.ms[("carr",)] = here("car")
                    E("vector", lambda e: e.memset(hin[:], 0.0), inc=("v", 1))
                    for r in range(NCORE):
                        E("vector", (lambda r=r: (lambda e: e.tensor_tensor(
                            out=wk16a[:], in0=carr[:, r * 32:r * 32 + 16],
                            in1=msk_sb[:, r * 16:(r + 1) * 16], op=ALU.mult)))(),
                          waits=[("carr",), ("ldall",)], inc=("v", 1))
                        E("vector", (lambda r=r: (lambda e: e.tensor_tensor(
                            out=wk16a[:], in0=wk16a[:], in1=imsk_sb[:, r * 16:(r + 1) * 16],
                            op=ALU.add)))(),
                          inc=("v", 1))
                        E("vector", lambda e: e.tensor_tensor(out=hin[:], in0=hin[:], in1=wk16a[:],
                                                              op=ALU.mult),
                          inc=("v", 1))
                        E("vector", (lambda r=r: (lambda e: e.tensor_tensor(
                            out=wk16b[:], in0=carr[:, r * 32 + 16:r * 32 + 32],
                            in1=msk_sb[:, r * 16:(r + 1) * 16], op=ALU.mult)))(),
                          inc=("v", 1))
                        E("vector", lambda e: e.tensor_tensor(out=hin[:], in0=hin[:], in1=wk16b[:],
                                                              op=ALU.add),
                          inc=("v", 1))
                    ctx.ms[("hin",)] = ("v", ctx.ctr["v"])

                def p5_step(j):
                    for h2 in range(2):
                        dst = bc[j % 3][0:64, :] if h2 == 0 else bc[j % 3][64:128, :]
                        E("sync", (lambda dst=dst, j=j, h2=h2: (lambda e: e.dma_start(
                            out=dst, in_=csp_d[2 * j + h2:2 * j + h2 + 1, :].broadcast_to([64, TC]))))(),
                          waits=[("cspst",),
                                 ("pexp", j - 3) if j >= 3 else ("aexp", NE - 3 + j)],
                          inc=(f"bc{j % 3}s", 16))
                    ctx.ms[("bcp", j)] = here(f"bc{j % 3}s")
                    E("scalar", (lambda j=j: (lambda e: e.activation(
                        out=a_sc[j % 2][:], in_=bc[j % 3][:], func=AF.Exp,
                        scale=acont[:, j:j + 1])))(),
                      waits=[("bcp", j),
                             ("corr", j - 2) if j >= 2 else ("scan", NE - 2 + j)],
                      inc=("s", 1), ms=("pexp", j))
                    E("vector", (lambda j=j: (lambda e: e.scalar_tensor_tensor(
                        out=scr[0][:], in0=a_sc[j % 2][:], scalar=hin[:, j:j + 1],
                        in1=u_sb[:, j * TC:(j + 1) * TC], op0=ALU.mult, op1=ALU.add)))(),
                      waits=[("pexp", j), ("hin",), ("ysconv", j - 1) if j >= 1 else None],
                      inc=("v", 1), ms=("corr", j))
                    E("scalar", (lambda j=j: (lambda e: e.copy(
                        out=ysbf[j % 2][:], in_=scr[0][:])))(),
                      waits=[("corr", j), ("ystst", j - 2) if j >= 2 else None],
                      inc=("s", 1), ms=("ysconv", j))
                    E("sync", (lambda j=j: (lambda e: e.dma_start(
                        out=yst_d[j * 128:(j + 1) * 128, :], in_=ysbf[j % 2][:])))(),
                      waits=[("ysconv", j)], inc=(f"yb{j % 2}", 16), ms=("ystst", j))

                p3j = [0]
                p5j = [0]
                for k in range(NI + 2):
                    c = k - 2
                    if c == 6:
                        p4_exchange()
                    i = k + 2
                    if i < NI:
                        p6_load(i)
                    if 0 <= c < NI:
                        p6_compute(c)
                        if c <= 3:
                            for _ in range(4):
                                if p3j[0] < NE:
                                    p3_step(p3j[0])
                                    p3j[0] += 1
                        if c in (9, 11, 13, 15):
                            for _ in range(4):
                                if p5j[0] < NE:
                                    p5_step(p5j[0])
                                    p5j[0] += 1
                pgpu.close()

        if _PH <= 5:
            reg_tap("dbg_u", u_sb[:], [128, NE * TC], F32, ("scan", NE - 1))
            reg_tap("dbg_yst", yst_d, [D, TC], BF16, ("ystst", NE - 1))
            reg_tap("dbg_hl", hl_sb[:], [128, NE], F32, ("scan", NE - 1))
            reg_tap("dbg_achk", achk[:], [128, NE], F32, ("achk",))
            reg_tap("dbg_hin", hin[:], [128, NE], F32, ("hin",))
            reg_tap("dbg_carr", carr[:], [128, 256], F32, ("carr",))
            reg_tap("dbg_sp", sp_d, [NH, TC], F32, ("spst",))
            reg_tap("dbg_csp", csp_d, [NH, TC], F32, ("cspst",))
            reg_tap("dbg_xnt", xnT[:], [128, NE * TC], BF16, ("xnT",))
            return finish()

        b_outer = ExitStack()
        b_outer_holder[0] = b_outer
        with ExitStack() as b1:

            gu_stack.close()
            gu_holder[0] = None
            xnT_stack.close()
            xnT_holder[0] = None
            g_tm = b_outer.enter_context(nc.sbuf_tensor("g_tm", [128, NT * D], BF16, side="right"))
            ysT2 = b_outer.enter_context(nc.sbuf_tensor("ysT2", [128, NE * TC], BF16, side="right"))
            woqbR = b_outer.enter_context(nc.sbuf_tensor("woqbR", [128, NE * 512], BF16, side="right"))
            woqbL = [None, None]

            def wo_buf(ch):
                return woqbR if ch == 0 else woqbL[ch % 2]

            wo_pos = [0]

            def wo_trickle(n):
                for _ in range(n):
                    dd = wo_pos[0]
                    if dd >= NE:
                        return
                    wo_pos[0] += 1
                    E("gpsimd", (lambda dd=dd: (lambda e: e.dma_start(
                        out=woqbR[:, dd * 512:(dd + 1) * 512],
                        in_=ap["wo"][0, dd, :, :])))(),
                      waits=[("gumm", NI - 1) if dd == 0 else None],
                      inc=("wq0", 16))
                    if dd == NE - 1:
                        ctx.ms[("woqd", 0)] = here("wq0")

            def wo_load(ch):
                for dd in range(NE):
                    E("gpsimd", (lambda ch=ch, dd=dd: (lambda e: e.dma_start(
                        out=woqbL[ch % 2][:, dd * 512:(dd + 1) * 512],
                        in_=ap["wo"][ch, dd, :, :])))(),
                      waits=[("ymm", ch - 2) if (ch >= 2 and dd == 0) else None,
                             ("downmm", NE - 1) if (ch == 1 and dd == 0) else None],
                      inc=(f"wq{ch % 2}", 16))
                ctx.ms[("woqd", ch)] = here(f"wq{ch % 2}")
            for dd in range(NE):
                E("sync", (lambda dd=dd: (lambda e: e.dma_start(
                    out=ysT2[:, dd * TC:(dd + 1) * TC],
                    in_=yst_d[dd * 128:(dd + 1) * 128, :])))(),
                  waits=[("gumm", NI - 1) if dd == 0 else None,
                         ("ystst", NE - 2) if dd == 0 else None,
                         ("ystst", NE - 1) if dd == 0 else None],
                  inc=("ysl", 16))
            ctx.ms[("yst2all",)] = here("ysl")

            with ExitStack() as b2:
                dwb = [b2.enter_context(nc.sbuf_tensor(f"dwb{b}", [128, IT], BF16)) for b in range(2)]
                gt_sc = [b2.enter_context(nc.sbuf_tensor(f"gt{b}", [128, TC], BF16)) for b in range(2)]
                with nc.psum_tensor("pd0", [128, TC], F32) as pd0, \
                     nc.psum_tensor("pd1", [128, TC], F32) as pd1:
                    pd = [pd0, pd1]
                    for Et in range(NE):
                        for q in range(2):
                            E("gpsimd", (lambda Et=Et, q=q: (lambda e: e.dma_start(
                                out=dwb[Et % 2][:, q * (IT // 2):(q + 1) * (IT // 2)],
                                in_=ap["dw"][Et, :, q * (IT // 2):(q + 1) * (IT // 2)])))(),
                              waits=[("downmm", Et - 2) if Et >= 2 else None,
                                     ("ystst", NE - 2) if Et == 0 and q == 0 else None,
                                     ("ystst", NE - 1) if Et == 0 and q == 0 else None],
                              inc=(f"dw{Et % 2}", 16))
                        ctx.ms[("dwdma2", Et)] = here(f"dw{Et % 2}")
                        if Et >= 2:
                            wo_trickle(4)
                        for nh in range(2):
                            for i in range(NI):
                                first = (nh == 0 and i == 0)
                                last = (nh == 1 and i == NI - 1)
                                E("tensor", (lambda Et=Et, nh=nh, i=i: (lambda e: e.matmul(
                                    out=pd[Et % 2][:, nh * 512:(nh + 1) * 512],
                                    lhsT=dwb[Et % 2][:, i * 128:(i + 1) * 128],
                                    rhs=hdn_slice(i)[:, nh * 512: nh * 512 + 512],
                                    start=(i == 0), stop=(i == NI - 1))))(),
                                  waits=([("dwdma2", Et), ("hdn", NI - 1),
                                          ("gt", Et - 2) if Et >= 2 else None]
                                         if first else []),
                                  inc=(("t", 1) if last else None),
                                  ms=(("downmm", Et) if last else None))
                        E("scalar", (lambda Et=Et: (lambda e: e.activation(
                            out=gt_sc[Et % 2][:], in_=pd[Et % 2][:], func=AF.Silu,
                            scale=ds_sb[:, Et:Et + 1])))(),
                          waits=[("downmm", Et), ("ldall",),
                                 ("xbar", Et - 2) if Et >= 2 else None],
                          inc=("s", 1), ms=("gt", Et))
                        for tt in range(NT):
                            E("sync", (lambda Et=Et, tt=tt: (lambda e: e.dma_start_transpose(
                                out=g_tm[:, tt * D + Et * 128: tt * D + (Et + 1) * 128],
                                in_=gt_sc[Et % 2][:, tt * 128:(tt + 1) * 128])))(),
                              waits=[("gt", Et)], inc=(f"xb{Et % 2}", 16))
                        ctx.ms[("xbar", Et)] = here(f"xb{Et % 2}")

        u_stack.close()

        if _PH <= 7:
            reg_tap("dbg_hdna", hdnT_a[:], [128, NSPLIT * TC], BF16, ("hdn", NI - 1))
            reg_tap("dbg_hdnb", u_bf[:, 0:(NI - NSPLIT) * TC], [128, (NI - NSPLIT) * TC], BF16, ("hdn", NI - 1))
            reg_tap("dbg_dwb", dwb[(NE - 1) % 2][:], [128, IT], BF16, ("dwdma2", NE - 1))
            reg_tap("dbg_gtm", g_tm[:], [128, NT * D], BF16, ("xbar", NE - 1))
            reg_tap("dbg_yst", yst_d, [D, TC], BF16, ("ystst", NE - 1))
            reg_tap("dbg_hin", hin[:], [128, NE], F32, ("hin",))
            return finish()

        with ExitStack() as c1:
            woqbL[0] = c1.enter_context(nc.sbuf_tensor("woqbL0", [128, NE * 512], BF16))
            woqbL[1] = c1.enter_context(nc.sbuf_tensor("woqbL1", [128, NE * 512], BF16))
            x2c = [c1.enter_context(nc.sbuf_tensor(f"x2c{b}", [128, 512], F32)) for b in range(2)]
            sg_sc = [c1.enter_context(nc.sbuf_tensor(f"sgs{b}", [128, 512], F32)) for b in range(2)]
            outc = [c1.enter_context(nc.sbuf_tensor(f"outc{b}", [128, 512], F32)) for b in range(4)]
            with ExitStack() as cps:
                py = [cps.enter_context(nc.psum_tensor(f"py{tt}", [128, 512], F32))
                      for tt in range(NT)]
                wo_load(1)
                for ch in range(4):
                    if ch + 1 >= 2 and ch + 1 < 4:
                        wo_load(ch + 1)
                    for tt in range(NT):
                        lin = ch * NT + tt
                        for dd in range(NE):
                            first = (dd == 0)
                            last = (dd == NE - 1)
                            E("tensor", (lambda ch=ch, dd=dd, tt=tt: (lambda e: e.matmul(
                                out=py[tt][:],
                                lhsT=ysT2[:, dd * TC + tt * 128: dd * TC + (tt + 1) * 128],
                                rhs=wo_buf(ch)[:, dd * 512:(dd + 1) * 512],
                                start=(dd == 0), stop=(dd == NE - 1))))(),
                              waits=([("woqd", ch),
                                      ("yst2all",),
                                      ("gt", NE - 1) if lin == 0 else None,
                                      ("sgmul", ch - 1, tt) if ch >= 1 else None]
                                     if first else []),
                              inc=(("t", 1) if last else None),
                              ms=(("ymmT", ch, tt) if last else None))
                        E("gpsimd", (lambda ch=ch, tt=tt, lin=lin: (lambda e: e.dma_start(
                            out=x2c[lin % 2][:],
                            in_=ap["x"][tt * 128:(tt + 1) * 128, ch * 512:(ch + 1) * 512])))(),
                          waits=[("sgst", (lin - 2) // NT, (lin - 2) % NT) if lin >= 2 else
                                 ("downmm", NE - 1),
                                 ("gt", NE - 1) if lin == 0 else None],
                          inc=(f"x2{lin % 2}", 16), ms=("x2", ch, tt))
                        E("vector", (lambda ch=ch, tt=tt: (lambda e: e.tensor_tensor(
                            out=sg_sc[tt % 2][:], in0=py[tt][:],
                            in1=g_tm[:, tt * D + ch * 512: tt * D + ch * 512 + 512],
                            op=ALU.mult)))(),
                          waits=[("ymmT", ch, tt),
                                 ("xbar", NE - 2), ("xbar", NE - 1),
                                 ("sgst", (lin - 2) // NT, (lin - 2) % NT) if lin >= 2 else None],
                          inc=("v", 1), ms=("sgmul", ch, tt))
                        E("vector", (lambda ch=ch, tt=tt, lin=lin: (lambda e: e.tensor_tensor(
                            out=outc[tt % 4][:], in0=sg_sc[tt % 2][:], in1=x2c[lin % 2][:],
                            op=ALU.add)))(),
                          waits=[("x2", ch, tt),
                                 ("outst", (lin - 4) // NT, (lin - 4) % NT) if lin >= 4 else None],
                          inc=("v", 1), ms=("sgst", ch, tt))
                        E("sync", (lambda ch=ch, tt=tt: (lambda e: e.dma_start(
                            out=ap["out"][tt * 128:(tt + 1) * 128, ch * 512:(ch + 1) * 512],
                            in_=outc[tt % 4][:])))(),
                          waits=[("sgst", ch, tt)], inc=(f"os{tt % 4}", 16), ms=("outst", ch, tt))
                    ctx.ms[("ymm", ch)] = ctx.ms[("ymmT", ch, NT - 1)]

        if DEBUG:
            reg_tap("dbg_yst", yst_d, [D, TC], BF16, ("ystst", NE - 1))
            reg_tap("dbg_gtm", g_tm[:], [128, NT * D], BF16, ("xbar", NE - 1))
            reg_tap("dbg_hin", hin[:], [128, NE], F32, ("hin",))
            reg_tap("dbg_achk", achk[:], [128, NE], F32, ("achk",))
            reg_tap("dbg_hl", hl_sb[:], [128, NE], F32, ("scan", NE - 1))
            reg_tap("dbg_carr", carr[:], [128, 256], F32, ("carr",))
            reg_tap("dbg_sp", sp_d, [NH, TC], F32, ("spst",))
            reg_tap("dbg_csp", csp_d, [NH, TC], F32, ("cspst",))
        return finish()


def _prep_inputs(kw):
    f = np.ascontiguousarray
    W_in = np.asarray(kw["L0_attn_W_in"], np.float32)
    W_out = np.asarray(kw["L0_attn_W_out"], np.float32)
    log_A = np.asarray(kw["L0_attn_log_A"], np.float32)
    W_dt = np.asarray(kw["L0_attn_W_dt"], np.float32)
    b_dt = np.asarray(kw["L0_attn_b_dt"], np.float32)
    gate_w = np.asarray(kw["L0_ffn_gate_w"], np.float32)
    up_w = np.asarray(kw["L0_ffn_up_w"], np.float32)
    down_w = np.asarray(kw["L0_ffn_down_w"], np.float32)
    gate_s = np.asarray(kw["L0_ffn_gate_s"], np.float32)
    up_s = np.asarray(kw["L0_ffn_up_s"], np.float32)
    down_s = np.asarray(kw["L0_ffn_down_s"], np.float32)

    com = {
        "w1": f(W_in.reshape(NE, 128, NE, 128).transpose(0, 3, 2, 1).reshape(NE, 128, D)),
        "wdt": f(W_dt.T.reshape(NE, 128, NH)),
        "bdt": f(b_dt.reshape(NH, 1)),
        "alog": f(log_A.reshape(NE, 128).T),
        "gw": f(gate_w.reshape(NI, 128, NE, 128).transpose(0, 3, 2, 1).reshape(NI, 128, D)),
        "uw": f(up_w.reshape(NI, 128, NE, 128).transpose(0, 3, 2, 1).reshape(NI, 128, D)),
        "dw": f(down_w.reshape(NE, 128, NI, 128).transpose(0, 3, 2, 1).reshape(NE, 128, IT)),
        "wo": f(W_out.reshape(4, 512, NE, 128).transpose(0, 2, 3, 1)),
        "gsc": f(gate_s.reshape(NI, 128).T),
        "usc": f(up_s.reshape(NI, 128).T),
        "dsc": f(down_s.reshape(NE, 128).T),
        "ident": np.eye(128, dtype=np.float32),
    }
    x_full = np.asarray(kw["L0_attn_x"], np.float32).reshape(NCORE * TC, D)
    in_maps = []
    for r in range(NCORE):
        m = np.zeros((128, 128), np.float32)
        for j in range(NCORE):
            if j < r:
                m[:, j * 16:(j + 1) * 16] = 1.0
        im = 1.0 - m
        d = dict(com)
        d["x"] = f(x_full[r * TC:(r + 1) * TC, :])
        d["msk"] = m
        d["imsk"] = im
        in_maps.append(d)
    return in_maps


_CACHE = {}
DEBUG = False
_PH = 8


def kernel(**inputs):
    in_maps = _prep_inputs(inputs)
    key = ("nc", DEBUG, _PH)
    if key not in _CACHE:
        _, ctx1 = build(dry=True)
        _CACHE[key] = build(dry=False, M=ctx1.ms)[0]
    nc = _CACHE[key]
    res = run_bass_kernel_spmd(nc, in_maps, core_ids=list(range(NCORE)))
    _CACHE["last_results"] = res.results
    out = np.concatenate([r["out"] for r in res.results], axis=0)
    return out.reshape(1, NCORE * TC, D).astype(np.float32)
